# version 29
# baseline (speedup 1.0000x reference)
import contextlib
import numpy as np
import ml_dtypes
import concourse.bass as bass
import concourse.mybir as mybir
from concourse.bass_utils import run_bass_kernel_spmd

F32 = mybir.dt.float32
BF16 = mybir.dt.bfloat16
I32 = mybir.dt.int32
AF = mybir.ActivationFunctionType
ALU = mybir.AluOpType

D = 1024
NCH = 8
HID = 3072
HCH = 24
NHEAD = 16
EPS_LN = 1e-5
EPS_ATT = 1e-6
SOFT_SHIFT = -20.0
INV_SQRT_D = 1.0 / 32.0
S_DOWN = 0.5 / (127.0 * 127.0)
INV127 = 1.0 / 127.0

ENGS = ("pe", "dve", "act", "pool", "sp")
ECTR = {"pe": "c_pe", "dve": "c_dve", "act": "c_act", "pool": "c_pool", "sp": "c_sp"}


class Prog:
    def __init__(self, nc, ctx):
        self.nc = nc
        self.q = {e: [] for e in ENGS}
        self.ctr = {}
        self.lastwait = {e: {} for e in ENGS}
        self.ctx = ctx

    def sem(self, name):
        h = self.ctx.enter_context(self.nc.semaphore(name))
        self.ctr[name] = [h, 0]

    def val(self, name):
        return self.ctr[name][1]

    def wait(self, eng, name, v=None):
        h, n = self.ctr[name]
        v = n if v is None else v
        if not v or v <= 0:
            return
        lw = self.lastwait[eng]
        if lw.get(name, -1) >= v:
            return
        lw[name] = v
        self.q[eng].append(lambda e, h=h, v=v: e.wait_ge(h, v))

    def _emit(self, eng, fn, inc, k):
        if inc is None:
            self.q[eng].append(fn)
            return None
        c = self.ctr[inc]
        c[1] += k
        h, v = c[0], c[1]
        self.q[eng].append(lambda e, fn=fn, h=h, k=k: fn(e).then_inc(h, k))
        return v

    def mm(self, o, l, r, start, stop, tp=None, inc=None):
        return self._emit("pe", lambda e, o=o, l=l, r=r, s=start, t=stop, tp=tp:
                          e.matmul(o, l, r, start=s, stop=t, tile_position=tp), inc, 1)

    def tp_(self, o, i, ident, inc=None):
        return self._emit("pe", lambda e, o=o, i=i, d=ident: e.transpose(o, i, d), inc, 1)

    def actv(self, o, i, func, bias=None, scale=None, accum=None, inc=None):
        kw = {}
        if bias is not None:
            kw["bias"] = bias
        if scale is not None:
            kw["scale"] = scale
        if accum is not None:
            kw["accum_out"] = accum
        return self._emit("act", lambda e, o=o, i=i, f=func, kw=kw:
                          e.activation(o, i, f, **kw), inc, 1)

    def tt(self, eng, o, a, b, op, inc=None):
        return self._emit(eng, lambda e, o=o, a=a, b=b, op=op:
                          e.tensor_tensor(o, a, b, op=op), inc, 1)

    def ts(self, eng, o, i, s1, s2, op0, op1=None, inc=None):
        if op1 is None:
            return self._emit(eng, lambda e, o=o, i=i, s1=s1:
                              e.tensor_scalar(o, i, s1, None, op0=op0), inc, 1)
        return self._emit(eng, lambda e, o=o, i=i, s1=s1, s2=s2:
                          e.tensor_scalar(o, i, s1, s2, op0=op0, op1=op1), inc, 1)

    def stt(self, o, i, s, b, op0, op1, inc=None):
        return self._emit("dve", lambda e, o=o, i=i, s=s, b=b, op0=op0, op1=op1:
                          e.scalar_tensor_tensor(o, i, s, b, op0=op0, op1=op1), inc, 1)

    def cp(self, eng, o, i, inc=None):
        return self._emit(eng, lambda e, o=o, i=i: e.tensor_copy(o, i), inc, 1)

    def recip(self, o, i, inc=None):
        return self._emit("dve", lambda e, o=o, i=i: e.reciprocal(o, i), inc, 1)

    def mset(self, eng, ap, v, inc=None):
        return self._emit(eng, lambda e, ap=ap, v=v: e.memset(ap, v), inc, 1)

    def dma(self, eng, o, i, sem, mdld=None, inc_k=16):
        kw = {} if mdld is None else {"max_dma_last_dim": mdld}
        return self._emit(eng, lambda e, o=o, i=i, kw=kw: e.dma_start(out=o, in_=i, **kw),
                          sem, inc_k)


class Res:
    def __init__(self, p):
        self.p = p
        self.w = {}
        self.r = {}

    def acquire(self, eng, keys):
        for b in keys:
            for s, v in self.r.get(b, ()):
                self.p.wait(eng, s, v)
            if b in self.w:
                s, v = self.w[b]
                self.p.wait(eng, s, v)

    def wrote(self, keys, val, sem="c_pe"):
        for b in keys:
            self.w[b] = (sem, val)
            self.r[b] = []

    def reading(self, eng, keys):
        for b in keys:
            if b in self.w:
                s, v = self.w[b]
                self.p.wait(eng, s, v)

    def read_done(self, keys, sem, val):
        for b in keys:
            self.r.setdefault(b, []).append((sem, val))


def build(T=2048):
    NT = T // 128
    SL = min(512, T); NSL = T // SL; TPSL = SL // 128
    SF = min(1024, T); NSF = T // SF
    SAT = min(512, T); NSA = T // SAT; NCK = SAT // 128
    NSC = T // 128
    SQ = SAT
    HHL = HCH // 2

    nc = bass.Bass("TRN2", target_bir_lowering=False)
    ctx = contextlib.ExitStack()
    p = Prog(nc, ctx)
    CAST_NAMES = ("gT0", "uT0", "dT0", "q0T", "k0T", "v0T", "o0T",
                  "gT1", "uT1", "dT1", "q1", "k1", "v1")
    for s in (("c_pe", "c_dve", "c_act", "c_pool", "c_sp", "d_in", "d_out",
               "d_i0", "d_i1", "d_i2", "d_i3",
               "q_f0", "q_f1", "q_wd", "q_wa", "q_wb", "q_hf", "dw_bvr",
               "q_s0", "q_s1", "q_s2", "q_s3", "q_o0", "q_o1", "q_o2", "q_o3",
               "dw_gT0b", "dw_uT0b", "dw_gT1b", "dw_uT1b", "dw_dT0b", "dw_dT1b",
               "dw_g0f", "dw_u0f")
              + tuple(f"dw_{nm}" for nm in CAST_NAMES)):
        p.sem(s)
    ps = Res(p)
    sbr = Res(p)
    XNA_TOK = {}

    def barrier():
        for e in ENGS:
            for f in ENGS:
                if f != e:
                    p.wait(e, ECTR[f])

    dt_ = nc.dram_tensor
    xT = dt_("xT", [D, T], F32, kind="ExternalInput")
    yT = dt_("yT", [D, T], F32, kind="ExternalOutput")
    identb = dt_("identb", [128, 128], BF16, kind="ExternalInput")
    maskb4 = dt_("maskb4", [128, 1024], BF16, kind="ExternalInput")
    onesc_h = dt_("onesc_h", [128, 1], BF16, kind="ExternalInput")
    bias1_h = dt_("bias1_h", [128, 24], F32, kind="ExternalInput")
    b1v_row_h = dt_("b1v_row_h", [1, D], F32, kind="ExternalInput")

    w_raw, w_b = {}, {}
    for l in (0, 1):
        for nm, shape in (("gT", [D, HID]), ("uT", [D, HID]), ("dT", [HID, D])):
            w_raw[f"{nm}{l}"] = dt_(f"{nm}{l}", shape, I32, kind="ExternalInput")
            w_b[f"{nm}{l}"] = dt_(f"{nm}{l}B", shape, BF16)
    for nm in ("q0T", "k0T", "v0T", "o0T", "q1", "k1", "v1"):
        w_raw[nm] = dt_(nm, [D, D], F32, kind="ExternalInput")
        w_b[nm] = dt_(f"{nm}B", [D, D], BF16)

    SA_COLS = max(NCH * T, HHL * SF, 16 * SAT + NCK * 1024 + 8 * SAT)
    SB_COLS = max(SF, 2 * NCK * 1024 + 1024 + 8 * SAT + 1040 + 16, NSC * 1024)
    SC_COLS = max(12288, 8 * SAT + 1024 + 16 * SAT, 8 * SQ + NSC * SQ + 1024)
    sb = lambda name, shape, dt: ctx.enter_context(nc.sbuf_tensor(name, shape, dt))
    R = sb("R", [128, NCH * T], F32)
    XN = sb("XN", [128, NCH * T], BF16)
    SBA = sb("SBA", [128, SA_COLS], BF16)
    SBB = sb("SBB", [128, SB_COLS], BF16)
    SBC = sb("SBC", [128, SC_COLS], BF16)
    WBUF = sb("WBUF", [128, 8192], BF16)
    SM = sb("SM", [128, 112], F32)
    DTILE = sb("DTILE", [128, 1024], BF16)
    LN1S = sb("LN1S", [128, 128], BF16)
    IDN = sb("IDN", [128, 128], BF16)
    MSK = sb("MSK", [128, 1024], BF16)
    ONE = sb("ONE", [128, 1], BF16)
    BIA = sb("BIA", [128, 24], F32)

    SM_DT, SM_RQ = 0, 16
    SM_MSQ, SM_VAR, SM_RL = 32, 48, 64
    SM_C20 = 66
    SM_EPS = 68
    SM_EPSL = 70
    SM_RS, SM_BS = 80, 96

    PA = ctx.enter_context(nc.psum_tensor("PA", [128, 1024], F32))
    PB = ctx.enter_context(nc.psum_tensor("PB", [128, 1024], F32))
    PC = ctx.enter_context(nc.psum_tensor("PC", [128, 1024], F32))
    PD = ctx.enter_context(nc.psum_tensor("PD", [128, 512], F32))
    PT = ctx.enter_context(nc.psum_tensor("PT", [128, 1024], BF16))
    PSUM_BASE = {"PA": 0, "PB": 2, "PC": 4, "PD": 6, "PT": 7}

    def banks(t, c0=0, c1=None):
        base = PSUM_BASE[t.name]
        if t.name in ("PD", "PT"):
            return (base,)
        c1 = 1024 if c1 is None else c1
        return tuple(range(base + c0 // 512, base + (c1 + 511) // 512))

    NIS = min(4, NSL)
    ISW = T // NIS
    for k4 in range(NIS):
        dstR = bass.AP(R, k4 * ISW, [[NCH * T, 128], [T, NCH], [1, ISW]])
        srcX = bass.AP(xT, k4 * ISW, [[T, 128], [128 * T, NCH], [1, ISW]])
        p.dma("sp", dstR, srcX, f"d_i{k4}")
    for dst, src in ((IDN, identb), (MSK, maskb4), (ONE, onesc_h), (BIA, bias1_h)):
        p.dma("sp", dst[:], src[:, :], "d_in")
    t_in = p.val("d_in")
    sbr.wrote(["R"], 16, f"d_i{NIS - 1}")
    sbr.wrote(["IDN", "MSK", "ONE", "BIA"], t_in, "d_in")
    first_ln = [True]

    p.mset("dve", SM[:, SM_C20:SM_C20 + 1], SOFT_SHIFT, inc="c_dve")
    sbr.wrote(["C20"], p.val("c_dve"), "c_dve")
    p.mset("dve", SM[:, SM_EPS:SM_EPS + 1], EPS_ATT, inc="c_dve")
    sbr.wrote(["EPSC"], p.val("c_dve"), "c_dve")
    p.mset("dve", SM[:, SM_EPSL:SM_EPSL + 1], EPS_LN, inc="c_dve")
    sbr.wrote(["EPSL"], p.val("c_dve"), "c_dve")
    p.mset("pool", LN1S[:], 1.0, inc="c_pool")
    sbr.wrote(["LN1S"], p.val("c_pool"), "c_pool")
    H2 = HID // 2
    cast_plan = [("gT0", None, 0, 256, "dw_g0f"), ("uT0", None, 0, 256, "dw_u0f"),
                 ("gT0", None, 256, H2 - 256, "dw_gT0"),
                 ("uT0", None, 256, H2 - 256, "dw_uT0"),
                 ("dT0", (0, HID // 2), 0, D, "dw_dT0"),
                 ("gT0", None, H2, H2, "dw_gT0b"), ("uT0", None, H2, H2, "dw_uT0b"),
                 ("dT0", (HID // 2, HID // 2), 0, D, "dw_dT0b"),
                 ("q0T", None, 0, D, "dw_q0T"), ("k0T", None, 0, D, "dw_k0T"),
                 ("v0T", None, 0, D, "dw_v0T"), ("o0T", None, 0, D, "dw_o0T"),
                 ("gT1", None, 0, H2, "dw_gT1"), ("uT1", None, 0, H2, "dw_uT1"),
                 ("dT1", (0, HID // 2), 0, D, "dw_dT1"),
                 ("gT1", None, H2, H2, "dw_gT1b"), ("uT1", None, H2, H2, "dw_uT1b"),
                 ("dT1", (HID // 2, HID // 2), 0, D, "dw_dT1b"),
                 ("q1", None, 0, D, "dw_q1"), ("k1", None, 0, D, "dw_k1"),
                 ("v1", None, 0, D, "dw_v1")]
    prev_cast = None
    for nm, rows, c0, cols, sem in cast_plan:
        if prev_cast:
            p.wait("pool", prev_cast, 16)
        if rows is None:
            p.dma("pool", w_b[nm][:, c0:c0 + cols], w_raw[nm][:, c0:c0 + cols],
                  sem, mdld=8192)
        else:
            r0, rn = rows
            p.dma("pool", w_b[nm][r0:r0 + rn, :], w_raw[nm][r0:r0 + rn, :],
                  sem, mdld=8192)
        prev_cast = sem

    ln_cast_done = {}
    ln_done = set()
    HOOKP = [False]
    APPLY_TOK = {}

    def ln_cast(k4, hooked=False):
        if k4 in ln_cast_done:
            return
        sbr.w.pop(f"XNa{k4}", None)
        sbr.r.pop(f"XNa{k4}", None)
        for c in range(NCH):
            XNA_TOK.pop((k4, c), None)
        csl = bass.AP(XN, k4 * SL, [[NCH * T, 128], [T, NCH], [1, SL]])
        rsl = bass.AP(R, k4 * SL, [[NCH * T, 128], [T, NCH], [1, SL]])
        e = {"sq": None, "ap": None, "keys": None}
        if hooked:
            p.wait("dve", "c_pe", p.val("c_pe"))
            p.wait("dve", "c_dve", p.val("c_dve"))
        else:
            sbr.reading("dve", ["R"])
            sbr.acquire("dve", ["XN"])
            if first_ln[0]:
                p.wait("dve", f"d_i{min(k4 * SL // ISW, NIS - 1)}", 16)
        e["xc"] = p.cp("dve", csl, rsl, inc="c_dve")
        sbr.wrote(["XN"], e["xc"], "c_dve")
        ln_cast_done[k4] = e
        if not hooked:
            ln_square(k4)

    def ln_square(k4):
        e = ln_cast_done[k4]
        sbr.reading("act", ["R"])
        sbr.acquire("act", ["SBA"])
        if first_ln[0]:
            p.wait("act", f"d_i{min(k4 * SL // ISW, NIS - 1)}", 16)
        rsl = bass.AP(R, k4 * SL, [[NCH * T, 128], [T, NCH], [1, SL]])
        qsl = bass.AP(SBA, k4 * SL, [[SA_COLS, 128], [T, NCH], [1, SL]])
        e["sq"] = p.actv(qsl, rsl, AF.Square, inc="c_act")
        e["ap"] = lambda c, tl: SBA[:, c * T + k4 * SL + tl * 128:
                                    c * T + k4 * SL + tl * 128 + 128]
        e["keys"] = ["SBA"]
        sbr.wrote(["SBA"], e["sq"], "c_act")

    def ln_hook_sq(k2, win, eng, r_tok=None):
        e = ln_cast_done[k2]
        if e["sq"] is not None:
            return
        buf, base, pitch, key = win
        p.wait(eng, "c_pe", p.val("c_pe"))
        p.wait(eng, "c_dve", r_tok if r_tok is not None else p.val("c_dve"))
        sbr.acquire(eng, [key])
        rsl = bass.AP(R, k2 * SL, [[NCH * T, 128], [T, NCH], [1, SL]])
        wsl = bass.AP(buf, base, [[pitch, 128], [SL, NCH], [1, SL]])
        if eng == "act":
            e["sq"] = p.actv(wsl, rsl, AF.Square, inc="c_act")
            e["sqsem"] = "c_act"
        else:
            e["sq"] = p.tt("dve", wsl, rsl, rsl, ALU.mult, inc="c_dve")
            e["sqsem"] = "c_dve"
        e["ap"] = lambda c, tl, buf=buf, base=base: \
            buf[:, base + c * SL + tl * 128: base + c * SL + tl * 128 + 128]
        e["keys"] = [key]
        sbr.wrote([key], e["sq"], e["sqsem"])

    def ln_full(k2):
        ln_stats(k2)
        apply_ln(k2)

    def ln_stats(k4):
        if k4 in ln_done or k4 not in ln_cast_done:
            return
        ln_done.add(k4)
        e = ln_cast_done[k4]
        if e["sq"] is None:
            ln_square(k4)
        ps.acquire("pe", banks(PD))
        sbr.reading("pe", ["ONE"])
        p.wait("pe", "c_dve", e["xc"])
        p.wait("pe", e.get("sqsem", "c_act"), e["sq"])
        t_stats = None
        for tl in range(TPSL):
            tt = k4 * TPSL + tl
            for c in range(NCH):
                t_stats = p.mm(PD[:, tt:tt + 1],
                               XN[:, c * T + tt * 128: c * T + tt * 128 + 128],
                               ONE[:], c == 0, c == NCH - 1,
                               inc="c_pe" if c == NCH - 1 else None) or t_stats
            for c in range(NCH):
                t_stats = p.mm(PD[:, NT + tt:NT + tt + 1], e["ap"](c, tl),
                               ONE[:], c == 0, c == NCH - 1,
                               inc="c_pe" if c == NCH - 1 else None) or t_stats
        sbr.read_done(e["keys"], "c_pe", t_stats)
        ps.wrote(banks(PD), t_stats)
        a0 = SM_MSQ + k4 * TPSL
        v0 = SM_VAR + k4 * TPSL
        r0 = SM_RS + k4 * TPSL
        b0 = SM_BS + k4 * TPSL
        d0 = k4 * TPSL
        if k4 in APPLY_TOK:
            ta_k, tb_k = APPLY_TOK.pop(k4)
            p.wait("act", "c_dve", ta_k)
            p.wait("dve", "c_act", tb_k)
        ps.reading("act", banks(PD))
        t1 = p.actv(SM[:, a0:a0 + TPSL], PD[:, d0:d0 + TPSL], AF.Square,
                    scale=1.0 / D, inc="c_act")
        ps.reading("dve", banks(PD))
        p.wait("dve", "c_act", t1)
        t2 = p.stt(SM[:, v0:v0 + TPSL], PD[:, NT + d0:NT + d0 + TPSL], 1.0 / D,
                   SM[:, a0:a0 + TPSL], ALU.mult, ALU.subtract, inc="c_dve")
        p.wait("act", "c_dve", t2)
        sbr.reading("act", ["EPSL"])
        t3 = p.actv(SM[:, a0:a0 + TPSL], SM[:, v0:v0 + TPSL], AF.Ln,
                    bias=SM[:, SM_EPSL:SM_EPSL + 1], inc="c_act")
        p.wait("act", "c_act", t3)
        t4 = p.actv(SM[:, r0:r0 + TPSL], SM[:, a0:a0 + TPSL], AF.Exp,
                    scale=-0.5, inc="c_act")
        p.wait("dve", "c_act", t4)
        t5 = p.stt(SM[:, b0:b0 + TPSL], PD[:, d0:d0 + TPSL], -1.0 / D,
                   SM[:, r0:r0 + TPSL], ALU.mult, ALU.mult, inc="c_dve")
        ps.read_done(banks(PD), "c_act", t1)
        ps.read_done(banks(PD), "c_dve", t5)
        sbr.wrote([f"LNRS{k4}"], t4, "c_act")
        sbr.wrote([f"LNBS{k4}"], t5, "c_dve")

    def phase_ln():
        for k4 in range(NSL):
            ln_cast(k4)
            if k4 > 0:
                ln_stats(k4 - 1)
                apply_ln(k4 - 1, pre=True)
        ln_stats(NSL - 1)
        apply_ln(NSL - 1, pre=True)
        first_ln[0] = False
        ln_done.clear()
        ln_cast_done.clear()

    def apply_ln(k, pre=False, pt=False):
        if f"XNa{k}" in sbr.w:
            return
        sbr.wrote([f"XNa{k}"], 0, "c_dve")
        if pt:
            ab = PT
            a_bk, a_ap = banks(PT), PT[:, 0:SL]
            b_bk, b_ap = banks(PT), PT[:, 512:512 + SL]
            b_off = 512
        elif pre:
            ab = (PA, PB)[k % 2]
            a_bk, a_ap = banks(ab, 0, 512), ab[:, 0:SL]
            b_bk, b_ap = banks(ab, 512, 1024), ab[:, 512:512 + SL]
            b_off = 512
        else:
            ab = None
            a_bk, a_ap = banks(PD), PD[:, 0:SL]
            b_bk, b_ap = banks(PC, 0, 512), PC[:, 0:SL]
            b_off = 0
        sbr.reading("dve", [f"LNRS{k}", "IDN"])
        sbr.reading("act", [f"LNBS{k}", "IDN"])
        ps.acquire("pe", a_bk + b_bk)
        sbr.acquire("dve", ["DTA"])
        sbr.acquire("act", ["DTB"])
        ta = tb = None
        for tl in range(TPSL):
            tt = k * TPSL + tl
            ta = p.ts("dve", DTILE[:, tl * 128:tl * 128 + 128], IDN[:],
                      SM[:, SM_RS + tt:SM_RS + tt + 1], None, ALU.mult, inc="c_dve")
            tb = p.actv(DTILE[:, 512 + tl * 128:512 + tl * 128 + 128], IDN[:],
                        AF.Copy, scale=SM[:, SM_BS + tt:SM_BS + tt + 1], inc="c_act")
        sbr.wrote(["DTA"], ta, "c_dve")
        sbr.wrote(["DTB"], tb, "c_act")
        APPLY_TOK[k] = (ta, tb)
        p.wait("pe", "c_dve", ta)
        p.wait("pe", "c_act", tb)
        sbr.reading("pe", ["LN1S"])
        b1 = b2 = None
        for tl in range(TPSL):
            b1 = p.mm((PD if not pre else ab)[:, tl * 128:tl * 128 + 128], LN1S[:],
                      DTILE[:, tl * 128:tl * 128 + 128], True, True, inc="c_pe")
            b2 = p.mm((PC if not pre else ab)[:, b_off + tl * 128:
                                              b_off + tl * 128 + 128], LN1S[:],
                      DTILE[:, 512 + tl * 128:512 + tl * 128 + 128], True, True,
                      inc="c_pe")
        sbr.read_done(["DTA", "DTB"], "c_pe", b2)
        ps.wrote(a_bk, b2)
        ps.wrote(b_bk, b2)
        ps.reading("act", a_bk + b_bk)
        tea = p.actv(DTILE[:, 0:SL], a_ap, AF.Copy, inc="c_act")
        teb = p.actv(DTILE[:, 512:512 + SL], b_ap, AF.Copy, inc="c_act")
        ps.read_done(a_bk + b_bk, "c_act", teb)
        sbr.wrote(["DTA"], tea, "c_act")
        sbr.wrote(["DTB"], teb, "c_act")
        sbr.reading("dve", ["DTA", "DTB", "XN"])
        last = None
        for c in range(NCH):
            sl = slice(c * T + k * SL, c * T + k * SL + SL)
            t1 = p.tt("dve", XN[:, sl], XN[:, sl], DTILE[:, 0:SL], ALU.mult,
                      inc="c_dve")
            p.wait("dve", "c_dve", t1)
            last = p.tt("dve", XN[:, sl], XN[:, sl], DTILE[:, 512:512 + SL],
                        ALU.add, inc="c_dve")
            XNA_TOK[(k, c)] = last
        sbr.read_done(["DTA", "DTB"], "c_dve", last)
        sbr.wrote([f"XNa{k}"], last, "c_dve")

    def pf_ffn(l):
        gB, uB, dB = w_b[f"gT{l}"], w_b[f"uT{l}"], w_b[f"dT{l}"]
        pairs = [(j, hf, hp) for j in range(NSF) for hf in range(2)
                 for hp in range(HHL // 2)]
        slab_tok = {}

        def issue_slab(i):
            if i >= len(pairs):
                return
            _, hf, hp = pairs[i]
            hpg = hf * (HHL // 2) + hp
            piece = "" if hpg < HHL // 2 else "b"
            if l == 0 and hpg == 0:
                p.wait("sp", "dw_g0f", 16)
                p.wait("sp", "dw_u0f", 16)
            else:
                p.wait("sp", f"dw_gT{l}{piece}", 16)
                p.wait("sp", f"dw_uT{l}{piece}", 16)
            slot = i % 2
            gcol, ucol = slot * 4096, slot * 4096 + 2048
            qf_sem = ("q_f0", "q_f1")[slot]
            sbr.acquire("sp", [f"WG{slot}", f"WU{slot}"] + ["WBUF", "WG0", "WU0", "WG1", "WU1"])
            p.dma("sp", WBUF[:, gcol:gcol + 2048].rearrange("p (c n) -> p c n", c=NCH),
                  gB[:, hpg * 256:hpg * 256 + 256]
                  .rearrange("(c p) n -> p c n", p=128), qf_sem)
            p.dma("sp", WBUF[:, ucol:ucol + 2048].rearrange("p (c n) -> p c n", c=NCH),
                  uB[:, hpg * 256:hpg * 256 + 256]
                  .rearrange("(c p) n -> p c n", p=128), qf_sem)
            slab_tok[i] = p.val(qf_sem)
            sbr.wrote([f"WG{slot}", f"WU{slot}"], slab_tok[i], qf_sem)

        wd_st = {"pend": [], "first": False, "piece": "", "tok": 0}

        def wd_queue(hf):
            wd_st["piece"] = "" if hf == 0 else "b"
            wd_st["first"] = True
            wd_st["pend"] = [0, 1, 2]

        def wd_drain(n):
            while n > 0 and wd_st["pend"]:
                dgl = wd_st["pend"].pop(0)
                if wd_st["first"]:
                    p.wait("sp", f"dw_dT{l}{wd_st['piece']}", 16)
                    sbr.acquire("sp", ["WD", "OTBUF", "AMT0", "AMT1", "KFH",
                                       "QS1", "PTR", "OTK"])
                    wd_st["first"] = False
                rows = (HHL * 128 if wd_st["piece"] else 0) + dgl * 512
                p.dma("sp",
                      SBC[:, dgl * 4096:dgl * 4096 + 4096]
                      .rearrange("p (c n) -> p c n", c=4),
                      dB[rows:rows + 512, :].rearrange("(c p) n -> p c n", p=128),
                      "q_wd")
                n -= 1
            if not wd_st["pend"]:
                wd_st["tok"] = p.val("q_wd")
                sbr.wrote(["WD"], wd_st["tok"], "q_wd")

        issue_slab(0)
        issue_slab(1)
        wd_queue(0)
        wd_drain(1)
        return {"issue_slab": issue_slab, "slab_tok": slab_tok,
                "wd_queue": wd_queue, "wd_drain": wd_drain, "wd_st": wd_st}

    def phase_ffn(l, st):
        issue_slab, slab_tok = st["issue_slab"], st["slab_tok"]
        wd_queue, wd_drain, wd_st = st["wd_queue"], st["wd_drain"], st["wd_st"]
        sbr.reading("pe", ["XN"])
        rot_state = [0]
        pi = 0
        pend_cast = []
        pend_sq = []
        pend_tr = [None]
        spj = max(1, SF // SL)

        def ffn_win(k2):
            return (SBB, 1024 + (k2 % spj) * NCH * SL, SB_COLS,
                    f"LNW{k2 % spj}")

        for j in range(NSF):
            for k in range(j * SF // SL, (j + 1) * SF // SL):
                apply_ln(k)
            for k2 in pend_cast:
                ln_cast(k2, hooked=True)
            pend_cast = []
            if pend_sq:
                ln_hook_sq(pend_sq[0], ffn_win(pend_sq[0]), "act", pend_tr[0])
            for hf in range(2):
                if not (j == 0 and hf == 0):
                    wd_queue(hf)
                for hp in range(HHL // 2):
                    slot = pi % 2
                    gcol, ucol = slot * 4096, slot * 4096 + 2048
                    qf_sem = ("q_f0", "q_f1")[slot]
                    t_slab = slab_tok[pi]
                    for hh in range(2):
                        hcl = hp * 2 + hh
                        p.wait("pe", qf_sem, t_slab)
                        for half in range(max(1, SF // 512)):
                            w = min(512, SF)
                            kk = (j * SF + half * 512) // SL
                            gb, go = ((PA, 0), (PA, 512), (PB, 0), (PB, 512))[
                                rot_state[0] % 4]
                            ub, uo = ((PC, 0), (PC, 512), (PD, 0))[rot_state[0] % 3]
                            rot_state[0] += 1
                            t_g = t_u = None
                            for col, pt, po in ((gcol, gb, go), (ucol, ub, uo)):
                                ps.acquire("pe", banks(pt, po, po + 512))
                                for dc in range(NCH):
                                    p.wait("pe", "c_dve", XNA_TOK.get((kk, dc), 0))
                                    tk = p.mm(
                                        pt[:, po:po + w],
                                        WBUF[:, col + dc * 256 + hh * 128:
                                             col + dc * 256 + hh * 128 + 128],
                                        XN[:, dc * T + j * SF + half * 512:
                                           dc * T + j * SF + half * 512 + w],
                                        dc == 0, dc == NCH - 1,
                                        inc="c_pe" if dc == NCH - 1 else None)
                                    if tk:
                                        if pt is gb and po == go:
                                            t_g = tk
                                        else:
                                            t_u = tk
                            sbr.read_done([f"WG{slot}", f"WU{slot}"], "c_pe", t_u)
                            ps.wrote(banks(gb, go, go + 512), t_g)
                            ps.wrote(banks(ub, uo, uo + 512), t_u)
                            ps.reading("act", banks(gb, go, go + 512))
                            sbr.acquire("act", [f"FTMP{half}"])
                            tsl = p.actv(SBB[:, half * 512:half * 512 + w],
                                         gb[:, go:go + w], AF.Silu, scale=INV127,
                                         inc="c_act")
                            sbr.wrote([f"FTMP{half}"], tsl, "c_act")
                            ps.read_done(banks(gb, go, go + 512), "c_act", tsl)
                            ps.reading("dve", banks(ub, uo, uo + 512))
                            p.wait("dve", "c_act", tsl)
                            sbr.acquire("dve", [f"H{hcl}_{half}"])
                            th = p.tt("dve",
                                      SBA[:, hcl * SF + half * 512:
                                          hcl * SF + half * 512 + w],
                                      SBB[:, half * 512:half * 512 + w],
                                      ub[:, uo:uo + w], ALU.mult, inc="c_dve")
                            sbr.wrote([f"H{hcl}_{half}"], th, "c_dve")
                            sbr.read_done([f"FTMP{half}"], "c_dve", th)
                            ps.read_done(banks(ub, uo, uo + 512), "c_dve", th)
                    issue_slab(pi + 2)
                    wd_drain(1)
                    if hf == 0 and pend_sq:
                        if hp == 0:
                            if len(pend_sq) > 1:
                                ln_hook_sq(pend_sq[1], ffn_win(pend_sq[1]),
                                           "act", pend_tr[0])
                            ln_full(pend_sq[0])
                        elif hp == 1:
                            for k2 in pend_sq[1:]:
                                ln_full(k2)
                            pend_sq = []
                    pi += 1
                hkeys = [f"H{hcl}_{hf2}" for hcl in range(HHL)
                         for hf2 in range(max(1, SF // 512))]
                wd_drain(3)
                sbr.reading("pe", ["WD"] + hkeys)
                p.wait("pe", "q_wd", wd_st["tok"])
                for do in range(NCH):
                    pt = (PA, PB)[do % 2]
                    ps.acquire("pe", banks(pt, 0, SF))
                    tk = None
                    for half in range(max(1, SF // 512)):
                        w = min(512, SF)
                        for hcl in range(HHL):
                            tk = p.mm(
                                pt[:, half * 512:half * 512 + w],
                                SBC[:, hcl * 1024 + do * 128: hcl * 1024 + do * 128 + 128],
                                SBA[:, hcl * SF + half * 512: hcl * SF + half * 512 + w],
                                hcl == 0, hcl == HHL - 1,
                                inc="c_pe" if hcl == HHL - 1 else None) or tk
                    ps.wrote(banks(pt, 0, SF), tk)
                    if do == NCH - 1:
                        sbr.read_done(["WD"] + hkeys, "c_pe", tk)
                    ps.reading("dve", banks(pt, 0, SF))
                    sbr.reading("dve", ["R"])
                    rsl = slice(do * T + j * SF, do * T + j * SF + SF)
                    tr = p.stt(R[:, rsl], pt[:, 0:SF], S_DOWN, R[:, rsl],
                               ALU.mult, ALU.add, inc="c_dve")
                    ps.read_done(banks(pt, 0, SF), "c_dve", tr)
                    if st.get("final") and j == NSF - 1 and hf == 1:
                        p.wait("sp", "c_dve", tr)
                        p.dma("sp", yT[do * 128:(do + 1) * 128, :],
                              R[:, do * T:(do + 1) * T], "d_out")
                        st["out_emitted"] = True
                if hf == 1 and HOOKP[0]:
                    pend_cast = list(range(j * spj, (j + 1) * spj))
                    pend_sq = list(pend_cast)
                    pend_tr[0] = tr
        for k2 in pend_cast:
            ln_cast(k2, hooked=True)
        for k2 in pend_sq:
            ln_hook_sq(k2, ffn_win(k2), "act", pend_tr[0])
        sbr.wrote(["R"], p.val("c_dve"), "c_dve")

    tmp_state = {"cur": 0}
    SPTMP = 2 * NCK * 1024

    def softplus_evac(pt, c0, w, dbuf, dcol):
        cur = tmp_state["cur"]
        tmp_state["cur"] ^= 1
        tcol = (SPTMP, SPTMP + 512)[cur]
        tkey = f"SPT{cur}"
        ps.reading("act", banks(pt, c0, c0 + w))
        sbr.acquire("act", [tkey])
        ta = p.actv(SBB[:, tcol:tcol + w], pt[:, c0:c0 + w], AF.Exp, inc="c_act")
        p.wait("act", "c_act", ta)
        tb = p.actv(SBB[:, tcol:tcol + w], SBB[:, tcol:tcol + w], AF.Ln, bias=1.0,
                    inc="c_act")
        sbr.wrote([tkey], tb, "c_act")
        ps.read_done(banks(pt, c0, c0 + w), "c_act", ta)
        p.wait("dve", "c_act", tb)
        tc = p.ts("dve", dbuf[:, dcol:dcol + w], SBB[:, tcol:tcol + w], EPS_ATT, None,
                  ALU.add, inc="c_dve")
        sbr.read_done([tkey], "c_dve", tc)
        return tc

    def load_w_striped(nm, dbuf=None, dbase=0, dstride=8192, sems="q_s",
                       keys=("WBUF", "WG0", "WU0", "WG1", "WU1"), per_stripe=False):
        dbuf = WBUF if dbuf is None else dbuf
        p.wait("sp", f"dw_{nm}", 16)
        if not per_stripe:
            sbr.acquire("sp", list(keys))
        toks = []
        for st4 in range(4):
            if per_stripe:
                sbr.acquire("sp", [f"WS{st4}"] + list(keys))
            dst = bass.AP(dbuf, dbase + st4 * 256, [[dstride, 128], [1024, NCH], [1, 256]])
            p.dma("sp", dst,
                  w_b[nm][:, st4 * 256:st4 * 256 + 256]
                  .rearrange("(c p) n -> p c n", p=128), f"{sems}{st4}")
            toks.append(p.val(f"{sems}{st4}"))
            if per_stripe:
                sbr.wrote([f"WS{st4}"], toks[-1], f"{sems}{st4}")
        sbr.wrote([keys[0]], toks[-1], f"{sems}3")
        return toks

    def pf_l0():
        return {"t_wq0": load_w_striped("q0T")}

    def phase_lattn(st):
        import os
        _cut = int(os.environ.get("MK_L0CUT", "0"))
        _scut = int(os.environ.get("MK_SCANCUT", "9"))
        QFV, KFV, KTOK, QFH_LO = 0, 8 * SAT, 16 * SAT, 16 * SAT + NCK * 1024
        VTK, OTOK = 0, NCK * 1024
        QFH_HI = SPTMP + 1024
        SPB = QFH_HI + 8 * SAT
        OT, AMT, KFH = 0, 8 * SAT, 8 * SAT + 1024

        def qfh_ap(h, c0, w):
            if h < 8:
                return SBA[0:64, QFH_LO + h * SAT + c0: QFH_LO + h * SAT + c0 + w]
            return SBB[0:64, QFH_HI + (h - 8) * SAT + c0: QFH_HI + (h - 8) * SAT + c0 + w]

        sbr.acquire("dve", ["SPB"])
        tz = p.mset("dve", SBB[0:64, SPB:SPB + 1040], 0.0, inc="c_dve")
        sbr.wrote(["SPB"], tz, "c_dve")
        sbr.reading("dve", ["MSK"])
        sbr.reading("pe", ["XN", "ONE", "IDN"])

        t_wq_next = st["t_wq0"]
        for j in range(NSA):
            apply_ln(j)
            if HOOKP[0] and j > 0:
                ln_cast(j - 1, hooked=True)
                ln_hook_sq(j - 1, (SBA, KTOK, SA_COLS, "KTB"), "act")
            t_wq = t_wq_next

            ROT6 = ((PA, 0), (PA, 512), (PB, 0), (PB, 512), (PC, 0), (PC, 512))

            def proj_feat_sp(t_w, dcol_base, wkey):
                tlast = None
                for ec in range(NCH):
                    bank, boff = ROT6[ec % 6]
                    ps.acquire("pe", banks(bank, boff, boff + 512))
                    p.wait("pe", f"q_s{ec // 2}", t_w[ec // 2])
                    tk = None
                    for dc in range(NCH):
                        p.wait("pe", "c_dve", XNA_TOK.get((j, dc), 0))
                        tk = p.mm(bank[:, boff:boff + SAT],
                                  WBUF[:, dc * 1024 + ec * 128:
                                       dc * 1024 + ec * 128 + 128],
                                  XN[:, dc * T + j * SAT: dc * T + j * SAT + SAT],
                                  dc == 0, dc == NCH - 1,
                                  inc="c_pe" if dc == NCH - 1 else None) or tk
                    sbr.read_done([wkey, f"WS{ec // 2}"], "c_pe", tk)
                    ps.wrote(banks(bank, boff, boff + 512), tk)
                    tlast = softplus_evac(bank, boff, SAT, SBA, dcol_base + ec * SAT)
                return tlast

            def headflat(srcbuf, srccol, dst_is_q, wkey):
                toks = []
                for half in range(2):
                    for par in range(2):
                        src = bass.AP(srcbuf, par * 64 * SA_COLS + srccol
                                      + half * 4 * SAT,
                                      [[SA_COLS, 64], [SAT, 4], [1, SAT]])
                        h0 = half * 8 + par
                        if dst_is_q:
                            if h0 < 8:
                                dbuf, dcol = SBA, QFH_LO + h0 * SAT
                                dstride = SA_COLS
                            else:
                                dbuf, dcol = SBB, QFH_HI + (h0 - 8) * SAT
                                dstride = SB_COLS
                        else:
                            dbuf, dcol = SBC, KFH + h0 * SAT
                            dstride = SC_COLS
                        dst = bass.AP(dbuf, dcol, [[dstride, 64], [2 * SAT, 4], [1, SAT]])
                        p.dma("sp", dst, src, "q_hf")
                        toks.append(p.val("q_hf"))
                return toks[-1]

            sbr.acquire("pe", ["QFV"])
            tqf = proj_feat_sp(t_wq, QFV, "WBUF")
            sbr.wrote(["QFV"], tqf, "c_dve")
            if HOOKP[0] and j > 0:
                ln_full(j - 1)
            t_wk = load_w_striped("k0T", per_stripe=True)
            sbr.acquire("pe", ["KFV"])
            tkf = proj_feat_sp(t_wk, KFV, "WBUF")
            sbr.wrote(["KFV"], tkf, "c_dve")
            sbr.reading("sp", ["QFV", "KFV"])
            sbr.acquire("sp", ["QFH", "KFH"])
            headflat(SBA, QFV, True, "QFV")
            t_hf = headflat(SBA, KFV, False, "KFV")
            sbr.wrote(["QFH", "KFH"], t_hf, "q_hf")
            sbr.read_done(["QFV", "KFV"], "q_hf", t_hf)
            sbr.acquire("pe", ["KTB"])
            tkt = None
            ri = 2
            for tl in range(NCK):
                for half in range(2):
                    bank, boff = ROT6[ri % 6]
                    ri += 1
                    ps.acquire("pe", banks(bank, boff, boff + 512))
                    p.wait("pe", f"q_s{2 * half}", t_wk[2 * half])
                    p.wait("pe", f"q_s{2 * half + 1}", t_wk[2 * half + 1])
                    tk = None
                    for dc in range(NCH):
                        p.wait("pe", "c_dve", XNA_TOK.get((j, dc), 0))
                        tk = p.mm(bank[:, boff:boff + 512],
                                  XN[:, dc * T + j * SAT + tl * 128:
                                     dc * T + j * SAT + tl * 128 + 128],
                                  WBUF[:, dc * 1024 + half * 512:
                                       dc * 1024 + half * 512 + 512],
                                  dc == 0, dc == NCH - 1,
                                  inc="c_pe" if dc == NCH - 1 else None) or tk
                    sbr.read_done(["WBUF", f"WS{2 * half}", f"WS{2 * half + 1}"],
                                  "c_pe", tk)
                    ps.wrote(banks(bank, boff, boff + 512), tk)
                    tkt = softplus_evac(bank, boff, 512, SBA,
                                        KTOK + tl * 1024 + half * 512)
            sbr.wrote(["KTB"], tkt, "c_dve")
            t_wv = load_w_striped("v0T", per_stripe=True)
            sbr.acquire("pe", ["VTK"])
            tv_d = tv_a = None
            for tl in range(NCK):
                for half in range(2):
                    bank, boff = ROT6[ri % 6]
                    ri += 1
                    ps.acquire("pe", banks(bank, boff, boff + 512))
                    p.wait("pe", f"q_s{2 * half}", t_wv[2 * half])
                    p.wait("pe", f"q_s{2 * half + 1}", t_wv[2 * half + 1])
                    tk = None
                    for dc in range(NCH):
                        p.wait("pe", "c_dve", XNA_TOK.get((j, dc), 0))
                        tk = p.mm(bank[:, boff:boff + 512],
                                  XN[:, dc * T + j * SAT + tl * 128:
                                     dc * T + j * SAT + tl * 128 + 128],
                                  WBUF[:, dc * 1024 + half * 512:
                                       dc * 1024 + half * 512 + 512],
                                  dc == 0, dc == NCH - 1,
                                  inc="c_pe" if dc == NCH - 1 else None) or tk
                    sbr.read_done(["WBUF", "WS0", "WS1", "WS2", "WS3"], "c_pe", tk)
                    ps.wrote(banks(bank, boff, boff + 512), tk)
                    dst = SBB[:, VTK + tl * 1024 + half * 512:
                              VTK + tl * 1024 + half * 512 + 512]
                    if ri % 2 == 0:
                        ps.reading("dve", banks(bank, boff, boff + 512))
                        tv_d = p.cp("dve", dst, bank[:, boff:boff + 512],
                                    inc="c_dve")
                        ps.read_done(banks(bank, boff, boff + 512), "c_dve", tv_d)
                    else:
                        ps.reading("act", banks(bank, boff, boff + 512))
                        tv_a = p.actv(dst, bank[:, boff:boff + 512], AF.Copy,
                                      inc="c_act")
                        ps.read_done(banks(bank, boff, boff + 512), "c_act", tv_a)
            sbr.wrote(["VTK"], tv_d, "c_dve")
            if tv_a is not None:
                p.wait("pe", "c_act", tv_a)
            big = (16 * SAT >= 8192)
            if big:
                t_wo = load_w_striped("o0T", dbuf=SBA, dbase=QFV, dstride=SA_COLS,
                                      sems="q_o", keys=("QFV", "KFV"))
            else:
                t_wo = load_w_striped("o0T", sems="q_o")
            if j + 1 < NSA:
                t_wq_next = load_w_striped("q0T", per_stripe=True)

            if _cut == 1:
                continue
            sbr.reading("pe", ["QFH", "KFH", "KTB", "VTK"])
            amt_tok = {}

            def emit_half(cl, half):
                for g2 in range(2):
                    acol = g2 * 512
                    ps.acquire("pe", banks(PA, acol, acol + 512))
                    ta = None
                    for hh in range(4):
                        h = half * 8 + g2 * 4 + hh
                        ta = p.mm(PA[:, acol + hh * 128:acol + hh * 128 + 128],
                                  SBC[0:64, KFH + h * SAT + cl * 128:
                                      KFH + h * SAT + cl * 128 + 128],
                                  qfh_ap(h, cl * 128, 128),
                                  True, True,
                                  inc="c_pe" if hh == 3 else None) or ta
                    ps.wrote(banks(PA, acol, acol + 512), ta)
                ps.reading("dve", banks(PA))
                sbr.acquire("dve", ["AMT"])
                tm = p.tt("dve", SBC[:, AMT:AMT + 1024], PA[:, 0:1024], MSK[:],
                          ALU.mult, inc="c_dve")
                sbr.wrote(["AMT"], tm, "c_dve")
                ps.read_done(banks(PA), "c_dve", tm)
                amt_tok[(cl, half)] = tm

            def emit_heads_half(cl, half):
                p.wait("pe", "c_dve", amt_tok[(cl, half)])
                sbr.reading("pe", ["SPB"])
                tl_ = None
                for hh in range(8):
                    h = half * 8 + hh
                    amc = AMT + hh * 128
                    ocol = 64 * h
                    vcol = VTK + cl * 1024 + h * 64
                    scol = SPB + h * 65
                    kct = KTOK + cl * 1024 + h * 64
                    st2 = _scut < 3
                    tl_ = p.mm(PB[:, ocol:ocol + 64], SBC[:, amc:amc + 128],
                               SBB[:, vcol:vcol + 64], True, st2,
                               inc="c_pe" if st2 else None) or tl_
                    tl_ = p.mm(PC[:, h:h + 1], SBC[:, amc:amc + 128],
                               ONE[:], True, st2,
                               inc="c_pe" if st2 else None) or tl_
                    if _scut >= 3:
                        tl_ = p.mm(PB[:, ocol:ocol + 64],
                                   qfh_ap(h, cl * 128, 128),
                                   SBB[0:64, scol:scol + 64], False, True,
                                   inc=None) or tl_
                        tl_ = p.mm(PC[:, h:h + 1],
                                   qfh_ap(h, cl * 128, 128),
                                   SBB[0:64, scol + 64:scol + 65], False, True,
                                   inc="c_pe" if hh == 7 and _scut < 4
                                   else None) or tl_
                    if _scut >= 4:
                        if h < 8:
                            dsb, dsc = PC, 512 + 64 * h
                        else:
                            dsb, dsc = PD, 64 * (h - 8)
                        tl_ = p.mm(dsb[0:64, dsc:dsc + 64],
                                   SBA[:, kct:kct + 64], SBB[:, vcol:vcol + 64],
                                   True, True, inc=None) or tl_
                        tl_ = p.mm(PC[0:64, 16 + h:16 + h + 1],
                                   SBA[:, kct:kct + 64], ONE[:],
                                   True, True,
                                   inc="c_pe" if hh == 7 else None) or tl_
                sbr.read_done(["AMT"], "c_pe", tl_)
                return tl_

            def emit_back(cl):
                ps.reading("act", banks(PC, 0, 512))
                sbr.acquire("act", ["DT", "RQ"])
                sbr.reading("act", ["EPSC"])
                td1 = p.actv(SM[:, SM_DT:SM_DT + 16], PC[:, 0:16], AF.Ln,
                             bias=SM[:, SM_EPS:SM_EPS + 1], inc="c_act")
                p.wait("act", "c_act", td1)
                td2 = p.actv(SM[:, SM_RQ:SM_RQ + 16], SM[:, SM_DT:SM_DT + 16], AF.Exp,
                             scale=-1.0, inc="c_act")
                sbr.wrote(["DT"], td1, "c_act")
                sbr.wrote(["RQ"], td2, "c_act")
                ps.reading("act", banks(PB, 0, 512))
                ps.reading("dve", banks(PB, 512, 1024))
                sbr.acquire("act", ["OTOK"])
                sbr.acquire("dve", ["OTOK2"])
                sbr.reading("act", ["RQ"])
                sbr.reading("dve", ["RQ"])
                toa = tod = None
                for h in range(NHEAD):
                    ocol = 64 * h
                    dcol = OTOK + cl * 1024 + h * 64
                    if h < 8:
                        toa = p.actv(SBB[:, dcol:dcol + 64], PB[:, ocol:ocol + 64],
                                     AF.Copy, scale=SM[:, SM_RQ + h:SM_RQ + h + 1],
                                     inc="c_act")
                    else:
                        tod = p.ts("dve", SBB[:, dcol:dcol + 64], PB[:, ocol:ocol + 64],
                                   SM[:, SM_RQ + h:SM_RQ + h + 1], None, ALU.mult,
                                   inc="c_dve")
                sbr.read_done(["RQ"], "c_act", toa)
                sbr.read_done(["RQ"], "c_dve", tod)
                ps.read_done(banks(PB, 0, 512), "c_act", toa)
                ps.read_done(banks(PB, 512, 1024), "c_dve", tod)
                sbr.wrote(["OTOK"], toa, "c_act")
                sbr.wrote(["OTOK2"], tod, "c_dve")
                ps.reading("dve", banks(PC, 512, 1024) + banks(PD) + banks(PC, 0, 512))
                sbr.acquire("dve", ["SPB"])
                ts1 = p.tt("dve",
                           bass.AP(SBB, SPB, [[SB_COLS, 64], [65, 8], [1, 64]]),
                           bass.AP(SBB, SPB, [[SB_COLS, 64], [65, 8], [1, 64]]),
                           bass.AP(PC, 512, [[1024, 64], [64, 8], [1, 64]]),
                           ALU.add, inc="c_dve")
                p.wait("dve", "c_dve", ts1)
                ts2 = p.tt("dve",
                           bass.AP(SBB, SPB + 8 * 65, [[SB_COLS, 64], [65, 8], [1, 64]]),
                           bass.AP(SBB, SPB + 8 * 65, [[SB_COLS, 64], [65, 8], [1, 64]]),
                           bass.AP(PD, 0, [[512, 64], [64, 8], [1, 64]]),
                           ALU.add, inc="c_dve")
                p.wait("dve", "c_dve", ts2)
                ts3 = p.tt("dve",
                           bass.AP(SBB, SPB + 64, [[SB_COLS, 64], [65, 16]]),
                           bass.AP(SBB, SPB + 64, [[SB_COLS, 64], [65, 16]]),
                           bass.AP(PC, 16, [[1024, 64], [1, 16]]),
                           ALU.add, inc="c_dve")
                ps.read_done(banks(PC) + banks(PD), "c_dve", ts3)
                sbr.wrote(["SPB"], ts3, "c_dve")

            def emit_tp(cl):
                ps.acquire("pe", banks(PT))
                sbr.reading("pe", ["OTOK", "OTOK2", "IDN"])
                tt6 = None
                for ec in range(NCH):
                    tt6 = p.tp_(PT[:, ec * 128:ec * 128 + 128],
                                SBB[:, OTOK + cl * 1024 + ec * 128:
                                    OTOK + cl * 1024 + ec * 128 + 128],
                                IDN[:],
                                inc="c_pe" if ec == NCH - 1 else None) or tt6
                sbr.read_done(["OTOK", "OTOK2"], "c_pe", tt6)
                ps.wrote(banks(PT), tt6)
                ps.reading("act", banks(PT))
                sbr.acquire("act", ["OTBUF"])
                dst = bass.AP(SBC, OT + cl * 128, [[SC_COLS, 128], [SAT, NCH], [1, 128]])
                src = bass.AP(PT, 0, [[1024, 128], [128, NCH], [1, 128]])
                tt7 = p.actv(dst, src, AF.Copy, inc="c_act")
                sbr.wrote(["OTBUF"], tt7, "c_act")
                ps.read_done(banks(PT), "c_act", tt7)

            if _scut >= 2:
                emit_half(0, 0)
                for cl in range(NCK):
                    ps.acquire("pe", banks(PB) + banks(PC) + banks(PD))
                    t1_ = emit_heads_half(cl, 0)
                    emit_half(cl, 1)
                    t2_ = emit_heads_half(cl, 1)
                    t_all = p.val("c_pe")
                    ps.wrote(banks(PB) + banks(PC) + banks(PD), t_all)
                    if cl + 1 < NCK:
                        emit_half(cl + 1, 0)
                    if _scut >= 5:
                        emit_back(cl)
                        emit_tp(cl)
            if _cut == 2 or _scut < 5:
                continue
            sbr.read_done(["QFH", "KFH", "KTB", "VTK", "SPB"], "c_pe", p.val("c_pe"))
            if _cut == 3:
                continue
            sbr.reading("pe", ["OTBUF"])
            for fc in range(NCH):
                bank, boff = ((PA, 0), (PA, 512), (PB, 0), (PB, 512))[fc % 4]
                p.wait("pe", f"q_o{fc // 2}", t_wo[fc // 2])
                ps.acquire("pe", banks(bank, boff, boff + 512))
                tk = None
                wot = SBA if big else WBUF
                wob = QFV if big else 0
                for ec in range(NCH):
                    tk = p.mm(bank[:, boff:boff + SAT],
                              wot[:, wob + ec * 1024 + fc * 128:
                                  wob + ec * 1024 + fc * 128 + 128],
                              SBC[:, OT + ec * SAT: OT + ec * SAT + SAT],
                              ec == 0, ec == NCH - 1,
                              inc="c_pe" if ec == NCH - 1 else None) or tk
                sbr.read_done((["QFV", "KFV"] if big else ["WBUF"]) + ["OTBUF"],
                              "c_pe", tk)
                ps.wrote(banks(bank, boff, boff + 512), tk)
                ps.reading("dve", banks(bank, boff, boff + 512))
                sbr.reading("dve", ["R"])
                rsl = slice(fc * T + j * SAT, fc * T + j * SAT + SAT)
                tr = p.tt("dve", R[:, rsl], bank[:, boff:boff + SAT], R[:, rsl],
                          ALU.add, inc="c_dve")
                ps.read_done(banks(bank, boff, boff + 512), "c_dve", tr)
        if HOOKP[0]:
            ln_cast(NSA - 1, hooked=True)
            ln_hook_sq(NSA - 1, (SBA, KTOK, SA_COLS, "KTB"), "act")
        sbr.wrote(["R"], p.val("c_dve"), "c_dve")

    def pf_l1():
        p.wait("sp", "dw_k1", 16)
        sbr.acquire("sp", ["WBUF", "WG0", "WU0", "WG1", "WU1"])
        p.dma("sp", WBUF[:, 0:8192].rearrange("p (c n) -> p c n", c=NCH),
              w_b["k1"][:, :].rearrange("(c p) n -> p c n", p=128), "q_wa")
        t = p.val("q_wa")
        sbr.wrote(["WBUF"], t, "q_wa")
        return {"t_wk": t}

    def phase_sattn(st):
        KT = 0
        VT = 0
        QS, PTR, OTK = 0, 8 * SQ, 8 * SQ + NSC * SQ
        sbr.reading("pe", ["XN", "ONE", "IDN"])
        sbr.reading("dve", ["BIA"])
        for k in range(NSL):
            apply_ln(k)
        t_wk = st["t_wk"]
        sbr.acquire("pe", ["KT1"])
        p.wait("pe", "q_wa", t_wk)
        KW = min(512, T)
        tkf = None
        kidx = 0
        for ec in range(NCH):
            for st in range(T // KW):
                bank, boff = ((PA, 0), (PA, 512), (PB, 0), (PB, 512))[kidx % 4]
                kidx += 1
                ps.acquire("pe", banks(bank, boff, boff + 512))
                tk = None
                for dc in range(NCH):
                    p.wait("pe", "c_dve", XNA_TOK.get((st, dc), 0))
                    tk = p.mm(bank[:, boff:boff + KW],
                              WBUF[:, dc * 1024 + ec * 128: dc * 1024 + ec * 128 + 128],
                              XN[:, dc * T + st * KW: dc * T + st * KW + KW],
                              dc == 0, dc == NCH - 1,
                              inc="c_pe" if dc == NCH - 1 else None) or tk
                sbr.read_done(["WBUF"], "c_pe", tk)
                ps.wrote(banks(bank, boff, boff + 512), tk)
                ps.reading("dve", banks(bank, boff, boff + 512))
                tkf = p.ts("dve",
                           SBA[:, KT + ec * T + st * KW: KT + ec * T + st * KW + KW],
                           bank[:, boff:boff + KW], BIA[:, 8 + ec:8 + ec + 1], None,
                           ALU.add, inc="c_dve")
                ps.read_done(banks(bank, boff, boff + 512), "c_dve", tkf)
        sbr.wrote(["KT1"], tkf, "c_dve")
        p.wait("sp", "dw_v1", 16)
        sbr.acquire("sp", ["WBUF"])
        p.dma("sp", WBUF[:, 0:8192].rearrange("p (c n) -> p c n", c=NCH),
              w_b["v1"][:, :].rearrange("(c p) n -> p c n", p=128), "q_wa")
        t_wv = p.val("q_wa")
        sbr.wrote(["WBUF"], t_wv, "q_wa")
        sbr.acquire("pe", ["VT1"])
        p.wait("pe", "q_wa", t_wv)
        tvt = None
        for tl in range(NSC):
            bank = (PB, PC)[tl % 2]
            ps.acquire("pe", banks(bank))
            tk = None
            for half in range(2):
                for dc in range(NCH):
                    p.wait("pe", "c_dve", XNA_TOK.get((tl * 128 // SL, dc), 0))
                    tk = p.mm(bank[:, half * 512:half * 512 + 512],
                              XN[:, dc * T + tl * 128: dc * T + tl * 128 + 128],
                              WBUF[:, dc * 1024 + half * 512:
                                   dc * 1024 + half * 512 + 512],
                              dc == 0, dc == NCH - 1,
                              inc="c_pe" if dc == NCH - 1 else None) or tk
            sbr.read_done(["WBUF"], "c_pe", tk)
            ps.wrote(banks(bank), tk)
            ps.reading("dve", banks(bank))
            tvt = p.cp("dve", SBB[:, VT + tl * 1024: VT + tl * 1024 + 1024],
                       bank[:, 0:1024], inc="c_dve")
            ps.read_done(banks(bank), "c_dve", tvt)
        sbr.wrote(["VT1"], tvt, "c_dve")
        p.wait("sp", "dw_q1", 16)
        sbr.acquire("sp", ["WBUF"])
        p.dma("sp", WBUF[:, 0:8192].rearrange("p (c n) -> p c n", c=NCH),
              w_b["q1"][:, :].rearrange("(c p) n -> p c n", p=128), "q_wa")
        t_wq = p.val("q_wa")
        sbr.wrote(["WBUF"], t_wq, "q_wa")

        for st in range(T // SQ):
            if HOOKP[0] and st > 0:
                ln_cast(st - 1, hooked=True)
                ln_hook_sq(st - 1, (SBC, PTR, SC_COLS, "PTR"), "act")
            sbr.acquire("pe", ["QS1"])
            p.wait("pe", "q_wa", t_wq)
            tqs = None
            for ec in range(NCH):
                bank, boff = ((PA, 0), (PA, 512), (PB, 0), (PB, 512))[ec % 4]
                ps.acquire("pe", banks(bank, boff, boff + 512))
                tk = None
                for dc in range(NCH):
                    p.wait("pe", "c_dve", XNA_TOK.get((st, dc), 0))
                    tk = p.mm(bank[:, boff:boff + SQ],
                              WBUF[:, dc * 1024 + ec * 128: dc * 1024 + ec * 128 + 128],
                              XN[:, dc * T + st * SQ: dc * T + st * SQ + SQ],
                              dc == 0, dc == NCH - 1,
                              inc="c_pe" if dc == NCH - 1 else None) or tk
                ps.wrote(banks(bank, boff, boff + 512), tk)
                ps.reading("dve", banks(bank, boff, boff + 512))
                tqs = p.ts("dve", SBC[:, QS + ec * SQ: QS + ec * SQ + SQ],
                           bank[:, boff:boff + SQ],
                           BIA[:, ec:ec + 1], None, ALU.add, inc="c_dve")
                ps.read_done(banks(bank, boff, boff + 512), "c_dve", tqs)
            sbr.wrote(["QS1"], tqs, "c_dve")
            if HOOKP[0] and st > 0:
                ln_full(st - 1)
            sbr.reading("pe", ["KT1", "QS1"])
            sbr.acquire("act", ["PTR"])
            tpt = None
            for sc in range(NSC):
                bank, boff = ((PA, 0), (PA, 512), (PB, 0), (PB, 512))[sc % 4]
                ps.acquire("pe", banks(bank, boff, boff + SQ))
                tk = None
                for ec in range(NCH):
                    tk = p.mm(bank[:, boff:boff + SQ],
                              SBA[:, KT + ec * T + sc * 128: KT + ec * T + sc * 128 + 128],
                              SBC[:, QS + ec * SQ: QS + ec * SQ + SQ],
                              ec == 0, ec == NCH - 1,
                              inc="c_pe" if ec == NCH - 1 else None) or tk
                ps.wrote(banks(bank, boff, boff + SQ), tk)
                ps.reading("act", banks(bank, boff, boff + SQ))
                sbr.reading("act", ["C20"])
                tpt = p.actv(SBC[:, PTR + sc * SQ: PTR + sc * SQ + SQ],
                             bank[:, boff:boff + SQ], AF.Exp,
                             bias=SM[:, SM_C20:SM_C20 + 1],
                             scale=INV_SQRT_D, inc="c_act")
                ps.read_done(banks(bank, boff, boff + SQ), "c_act", tpt)
            sbr.wrote(["PTR"], tpt, "c_act")
            sbr.reading("pe", ["PTR", "VT1"])
            NTL = SQ // 128
            pv_tok = [None] * NTL

            def emit_pv(tl):
                obank = (PC, PB)[tl % 2]
                lbank = (PD, PA)[tl % 2]
                ps.acquire("pe", banks(obank) + banks(lbank, 0, 512))
                tko = tkl = None
                for sc in range(NSC):
                    for half in range(2):
                        tko = p.mm(obank[:, half * 512:half * 512 + 512],
                                   SBC[:, PTR + sc * SQ + tl * 128:
                                       PTR + sc * SQ + tl * 128 + 128],
                                   SBB[:, VT + sc * 1024 + half * 512:
                                       VT + sc * 1024 + half * 512 + 512],
                                   sc == 0, sc == NSC - 1,
                                   inc="c_pe" if sc == NSC - 1 else None) or tko
                    tkl = p.mm(lbank[:, tl:tl + 1],
                               SBC[:, PTR + sc * SQ + tl * 128:
                                   PTR + sc * SQ + tl * 128 + 128],
                               ONE[:], sc == 0, sc == NSC - 1,
                               inc="c_pe" if sc == NSC - 1 else None) or tkl
                ps.wrote(banks(obank), tko)
                ps.wrote(banks(lbank, 0, 512), tkl)
                pv_tok[tl] = (obank, lbank, tko, tkl)

            def emit_tail(tl):
                obank, lbank, tko, tkl = pv_tok[tl]
                ps.reading("dve", banks(lbank, 0, 512))
                sbr.acquire("dve", ["RL"])
                tl1 = p.recip(SM[:, SM_RL:SM_RL + 1], lbank[:, tl:tl + 1], inc="c_dve")
                sbr.wrote(["RL"], tl1, "c_dve")
                ps.read_done(banks(lbank, 0, 512), "c_dve", tl1)
                ps.reading("dve", banks(obank))
                sbr.acquire("dve", ["OTK"])
                p.wait("dve", "c_dve", tl1)
                tdv = p.ts("dve", SBC[:, OTK:OTK + 1024], obank[:, 0:1024],
                           SM[:, SM_RL:SM_RL + 1], None, ALU.mult, inc="c_dve")
                sbr.wrote(["OTK"], tdv, "c_dve")
                sbr.read_done(["RL"], "c_dve", tdv)
                ps.read_done(banks(obank), "c_dve", tdv)
                ps.acquire("pe", banks(PT))
                p.wait("pe", "c_dve", tdv)
                tt6 = None
                for ec in range(NCH):
                    tt6 = p.tp_(PT[:, ec * 128:ec * 128 + 128],
                                SBC[:, OTK + ec * 128: OTK + ec * 128 + 128], IDN[:],
                                inc="c_pe" if ec == NCH - 1 else None) or tt6
                ps.wrote(banks(PT), tt6)
                sbr.read_done(["OTK"], "c_pe", tt6)
                ps.reading("dve", banks(PT))
                sbr.reading("dve", ["R", "BIA"])
                tra = None
                for ec in range(NCH):
                    rsl = slice(ec * T + st * SQ + tl * 128,
                                ec * T + st * SQ + tl * 128 + 128)
                    tra = p.stt(R[:, rsl], PT[:, ec * 128:ec * 128 + 128],
                                BIA[:, 16 + ec:16 + ec + 1], R[:, rsl],
                                ALU.add, ALU.add, inc="c_dve")
                ps.read_done(banks(PT), "c_dve", tra)

            emit_pv(0)
            for tl in range(NTL):
                if tl + 1 < NTL:
                    emit_pv(tl + 1)
                emit_tail(tl)
            sbr.read_done(["PTR", "VT1", "KT1", "QS1"], "c_pe", p.val("c_pe"))
        if HOOKP[0]:
            ln_cast(T // SQ - 1, hooked=True)
            ln_hook_sq(T // SQ - 1, (SBC, PTR, SC_COLS, "PTR"), "act")
        sbr.wrote(["R"], p.val("c_dve"), "c_dve")

    import os
    _ph = os.environ.get("MK_PHASES", "all")
    _plan = [("ln",), ("ffn", 0), ("ln",), ("l0",), ("ln",),
             ("ffn", 0), ("ln",), ("ffn", 1), ("ln",), ("l1",),
             ("ln",), ("ffn", 1)]
    if _ph != "all":
        _plan = _plan[:int(_ph)]
    _last_st = {}
    for _i, _step in enumerate(_plan):
        if _step[0] == "ln":
            phase_ln()
            _nxt = _plan[_i + 1] if _i + 1 < len(_plan) else None
            _st = None
            if _nxt and _nxt[0] == "ffn":
                _st = pf_ffn(_nxt[1])
                if _i + 2 == len(_plan):
                    _st["final"] = True
            elif _nxt and _nxt[0] == "l0":
                _st = pf_l0()
            elif _nxt and _nxt[0] == "l1":
                _st = pf_l1()
            barrier()
        elif _step[0] == "ffn":
            HOOKP[0] = _i + 1 < len(_plan) and _plan[_i + 1][0] == "ln"
            phase_ffn(_step[1], _st)
            _last_st = _st or {}
            barrier()
        elif _step[0] == "l0":
            HOOKP[0] = _i + 1 < len(_plan) and _plan[_i + 1][0] == "ln"
            phase_lattn(_st)
            barrier()
        else:
            HOOKP[0] = _i + 1 < len(_plan) and _plan[_i + 1][0] == "ln"
            phase_sattn(_st)
            barrier()

    if _last_st.get("out_emitted"):
        p.wait("sp", "d_out", 16 * NCH)
    else:
        sbr.reading("sp", ["R"])
        p.dma("sp", yT[:, :].rearrange("(c p) t -> p c t", p=128),
              R[:].rearrange("p (c t) -> p c t", c=NCH), "d_out")
        p.wait("sp", "d_out", 16)

    with nc.Block() as blk:
        @blk.tensor
        def _(t):
            for f in p.q["pe"]:
                f(t)

        @blk.vector
        def _(v):
            for f in p.q["dve"]:
                f(v)

        @blk.scalar
        def _(s):
            for f in p.q["act"]:
                f(s)

        @blk.gpsimd
        def _(g):
            for f in p.q["pool"]:
                f(g)

        @blk.sync
        def _(sy):
            for f in p.q["sp"]:
                f(sy)

    ctx.close()
    return nc


def _consts():
    bf = ml_dtypes.bfloat16
    identb = np.eye(128, dtype=np.float32).astype(bf)
    tri = np.triu(np.ones((128, 128), dtype=np.float32)).astype(bf)
    maskb4 = np.tile(tri, (1, 8))
    onesc = np.ones((128, 1), dtype=np.float32).astype(bf)
    return identb, maskb4, onesc


def kernel(L0_attn_x, L0_attn_Wq, L0_attn_Wk, L0_attn_Wv, L0_attn_Wo,
           L0_ffn_w_gate, L0_ffn_w_up, L0_ffn_w_down,
           L1_attn_Wq, L1_attn_bq, L1_attn_Wk, L1_attn_bk, L1_attn_Wv, L1_attn_bv,
           L1_ffn_w_gate, L1_ffn_w_up, L1_ffn_w_down):
    B, T, _ = L0_attn_x.shape
    nc = build(T)
    identb, maskb4, onesc = _consts()
    ct = np.ascontiguousarray
    bias1 = np.empty((128, 24), np.float32)
    bias1[:, 0:8] = np.asarray(L1_attn_bq, np.float32).reshape(8, 128).T
    bias1[:, 8:16] = np.asarray(L1_attn_bk, np.float32).reshape(8, 128).T
    bias1[:, 16:24] = np.asarray(L1_attn_bv, np.float32).reshape(8, 128).T
    common = {
        "identb": identb, "maskb4": maskb4, "onesc_h": onesc, "bias1_h": bias1,
        "b1v_row_h": np.asarray(L1_attn_bv, np.float32).reshape(1, D),
        "gT0": ct(np.asarray(L0_ffn_w_gate).T), "uT0": ct(np.asarray(L0_ffn_w_up).T),
        "dT0": ct(np.asarray(L0_ffn_w_down).T),
        "gT1": ct(np.asarray(L1_ffn_w_gate).T), "uT1": ct(np.asarray(L1_ffn_w_up).T),
        "dT1": ct(np.asarray(L1_ffn_w_down).T),
        "q0T": ct(np.asarray(L0_attn_Wq).T), "k0T": ct(np.asarray(L0_attn_Wk).T),
        "v0T": ct(np.asarray(L0_attn_Wv).T), "o0T": ct(np.asarray(L0_attn_Wo).T),
        "q1": ct(np.asarray(L1_attn_Wq)), "k1": ct(np.asarray(L1_attn_Wk)),
        "v1": ct(np.asarray(L1_attn_Wv)),
    }
    x = np.asarray(L0_attn_x, np.float32)
    in_maps = []
    for b in range(B):
        m = dict(common)
        m["xT"] = ct(x[b].T)
        in_maps.append(m)
    res = run_bass_kernel_spmd(nc, in_maps, core_ids=list(range(B)))
    out = np.empty((B, T, D), np.float32)
    for b in range(B):
        out[b] = np.asarray(res.results[b]["yT"]).T
    return out


# version 30
# speedup vs baseline: 1.0312x; 1.0312x over previous
import contextlib
import numpy as np
import ml_dtypes
import concourse.bass as bass
import concourse.mybir as mybir
from concourse.bass_utils import run_bass_kernel_spmd

F32 = mybir.dt.float32
BF16 = mybir.dt.bfloat16
I32 = mybir.dt.int32
AF = mybir.ActivationFunctionType
ALU = mybir.AluOpType

D = 1024
NCH = 8
HID = 3072
HCH = 24
NHEAD = 16
EPS_LN = 1e-5
EPS_ATT = 1e-6
SOFT_SHIFT = -20.0
INV_SQRT_D = 1.0 / 32.0
S_DOWN = 0.5 / (127.0 * 127.0)
INV127 = 1.0 / 127.0

ENGS = ("pe", "dve", "act", "pool", "sp")
ECTR = {"pe": "c_pe", "dve": "c_dve", "act": "c_act", "pool": "c_pool", "sp": "c_sp"}


class Prog:
    def __init__(self, nc, ctx):
        self.nc = nc
        self.q = {e: [] for e in ENGS}
        self.ctr = {}
        self.lastwait = {e: {} for e in ENGS}
        self.ctx = ctx

    def sem(self, name):
        h = self.ctx.enter_context(self.nc.semaphore(name))
        self.ctr[name] = [h, 0]

    def val(self, name):
        return self.ctr[name][1]

    def wait(self, eng, name, v=None):
        h, n = self.ctr[name]
        v = n if v is None else v
        if not v or v <= 0:
            return
        lw = self.lastwait[eng]
        if lw.get(name, -1) >= v:
            return
        lw[name] = v
        self.q[eng].append(lambda e, h=h, v=v: e.wait_ge(h, v))

    def _emit(self, eng, fn, inc, k):
        if inc is None:
            self.q[eng].append(fn)
            return None
        c = self.ctr[inc]
        c[1] += k
        h, v = c[0], c[1]
        self.q[eng].append(lambda e, fn=fn, h=h, k=k: fn(e).then_inc(h, k))
        return v

    def mm(self, o, l, r, start, stop, tp=None, inc=None):
        return self._emit("pe", lambda e, o=o, l=l, r=r, s=start, t=stop, tp=tp:
                          e.matmul(o, l, r, start=s, stop=t, tile_position=tp), inc, 1)

    def tp_(self, o, i, ident, inc=None):
        return self._emit("pe", lambda e, o=o, i=i, d=ident: e.transpose(o, i, d), inc, 1)

    def actv(self, o, i, func, bias=None, scale=None, accum=None, inc=None):
        kw = {}
        if bias is not None:
            kw["bias"] = bias
        if scale is not None:
            kw["scale"] = scale
        if accum is not None:
            kw["accum_out"] = accum
        return self._emit("act", lambda e, o=o, i=i, f=func, kw=kw:
                          e.activation(o, i, f, **kw), inc, 1)

    def tt(self, eng, o, a, b, op, inc=None):
        return self._emit(eng, lambda e, o=o, a=a, b=b, op=op:
                          e.tensor_tensor(o, a, b, op=op), inc, 1)

    def ts(self, eng, o, i, s1, s2, op0, op1=None, inc=None):
        if op1 is None:
            return self._emit(eng, lambda e, o=o, i=i, s1=s1:
                              e.tensor_scalar(o, i, s1, None, op0=op0), inc, 1)
        return self._emit(eng, lambda e, o=o, i=i, s1=s1, s2=s2:
                          e.tensor_scalar(o, i, s1, s2, op0=op0, op1=op1), inc, 1)

    def stt(self, o, i, s, b, op0, op1, inc=None):
        return self._emit("dve", lambda e, o=o, i=i, s=s, b=b, op0=op0, op1=op1:
                          e.scalar_tensor_tensor(o, i, s, b, op0=op0, op1=op1), inc, 1)

    def cp(self, eng, o, i, inc=None):
        return self._emit(eng, lambda e, o=o, i=i: e.tensor_copy(o, i), inc, 1)

    def recip(self, o, i, inc=None):
        return self._emit("dve", lambda e, o=o, i=i: e.reciprocal(o, i), inc, 1)

    def mset(self, eng, ap, v, inc=None):
        return self._emit(eng, lambda e, ap=ap, v=v: e.memset(ap, v), inc, 1)

    def dma(self, eng, o, i, sem, mdld=None, inc_k=16):
        kw = {} if mdld is None else {"max_dma_last_dim": mdld}
        return self._emit(eng, lambda e, o=o, i=i, kw=kw: e.dma_start(out=o, in_=i, **kw),
                          sem, inc_k)


class Res:
    def __init__(self, p):
        self.p = p
        self.w = {}
        self.r = {}

    def acquire(self, eng, keys):
        for b in keys:
            for s, v in self.r.get(b, ()):
                self.p.wait(eng, s, v)
            if b in self.w:
                s, v = self.w[b]
                self.p.wait(eng, s, v)

    def wrote(self, keys, val, sem="c_pe"):
        for b in keys:
            self.w[b] = (sem, val)
            self.r[b] = []

    def reading(self, eng, keys):
        for b in keys:
            if b in self.w:
                s, v = self.w[b]
                self.p.wait(eng, s, v)

    def read_done(self, keys, sem, val):
        for b in keys:
            self.r.setdefault(b, []).append((sem, val))


def build(T=2048):
    NT = T // 128
    SL = min(512, T); NSL = T // SL; TPSL = SL // 128
    SF = min(1024, T); NSF = T // SF
    SAT = min(512, T); NSA = T // SAT; NCK = SAT // 128
    NSC = T // 128
    SQ = SAT
    HHL = HCH // 2

    nc = bass.Bass("TRN2", target_bir_lowering=False)
    ctx = contextlib.ExitStack()
    p = Prog(nc, ctx)
    CAST_NAMES = ("gT0", "uT0", "dT0", "q0T", "k0T", "v0T", "o0T",
                  "gT1", "uT1", "dT1", "q1", "k1", "v1")
    for s in (("c_pe", "c_dve", "c_act", "c_pool", "c_sp", "d_in", "d_out",
               "d_i0", "d_i1", "d_i2", "d_i3",
               "q_f0", "q_f1", "q_wd", "q_wa", "q_wb", "q_hf", "dw_bvr",
               "q_s0", "q_s1", "q_s2", "q_s3", "q_o0", "q_o1", "q_o2", "q_o3",
               "dw_gT0b", "dw_uT0b", "dw_gT1b", "dw_uT1b", "dw_dT0b", "dw_dT1b",
               "dw_g0f", "dw_u0f")
              + tuple(f"dw_{nm}" for nm in CAST_NAMES)):
        p.sem(s)
    ps = Res(p)
    sbr = Res(p)
    XNA_TOK = {}

    def barrier():
        for e in ENGS:
            for f in ENGS:
                if f != e:
                    p.wait(e, ECTR[f])

    dt_ = nc.dram_tensor
    xT = dt_("xT", [D, T], F32, kind="ExternalInput")
    yT = dt_("yT", [D, T], F32, kind="ExternalOutput")
    identb = dt_("identb", [128, 128], BF16, kind="ExternalInput")
    maskb4 = dt_("maskb4", [128, 1024], BF16, kind="ExternalInput")
    onesc_h = dt_("onesc_h", [128, 1], BF16, kind="ExternalInput")
    bias1_h = dt_("bias1_h", [128, 24], F32, kind="ExternalInput")
    b1v_row_h = dt_("b1v_row_h", [1, D], F32, kind="ExternalInput")

    w_raw, w_b = {}, {}
    for l in (0, 1):
        for nm, shape in (("gT", [D, HID]), ("uT", [D, HID]), ("dT", [HID, D])):
            w_raw[f"{nm}{l}"] = dt_(f"{nm}{l}", shape, I32, kind="ExternalInput")
            w_b[f"{nm}{l}"] = dt_(f"{nm}{l}B", shape, BF16)
    for nm in ("q0T", "k0T", "v0T", "o0T", "q1", "k1", "v1"):
        w_raw[nm] = dt_(nm, [D, D], F32, kind="ExternalInput")
        w_b[nm] = dt_(f"{nm}B", [D, D], BF16)

    SA_COLS = max(NCH * T, HHL * SF, 16 * SAT + NCK * 1024 + 8 * SAT)
    SB_COLS = max(SF, 2 * NCK * 1024 + 1024 + 8 * SAT + 1040 + 16, NSC * 1024)
    SC_COLS = max(12288, 8 * SAT + 1024 + 16 * SAT, 8 * SQ + NSC * SQ + 1024)
    sb = lambda name, shape, dt: ctx.enter_context(nc.sbuf_tensor(name, shape, dt))
    R = sb("R", [128, NCH * T], F32)
    XN = sb("XN", [128, NCH * T], BF16)
    SBA = sb("SBA", [128, SA_COLS], BF16)
    SBB = sb("SBB", [128, SB_COLS], BF16)
    SBC = sb("SBC", [128, SC_COLS], BF16)
    WBUF = sb("WBUF", [128, 8192], BF16)
    SM = sb("SM", [128, 112], F32)
    DTILE = sb("DTILE", [128, 1024], BF16)
    LN1S = sb("LN1S", [128, 128], BF16)
    IDN = sb("IDN", [128, 128], BF16)
    MSK = sb("MSK", [128, 1024], BF16)
    ONE = sb("ONE", [128, 1], BF16)
    BIA = sb("BIA", [128, 24], F32)

    SM_DT, SM_RQ = 0, 16
    SM_MSQ, SM_VAR, SM_RL = 32, 48, 64
    SM_C20 = 66
    SM_EPS = 68
    SM_EPSL = 70
    SM_RS, SM_BS = 80, 96

    PA = ctx.enter_context(nc.psum_tensor("PA", [128, 1024], F32))
    PB = ctx.enter_context(nc.psum_tensor("PB", [128, 1024], F32))
    PC = ctx.enter_context(nc.psum_tensor("PC", [128, 1024], F32))
    PD = ctx.enter_context(nc.psum_tensor("PD", [128, 512], F32))
    PT = ctx.enter_context(nc.psum_tensor("PT", [128, 1024], BF16))
    PSUM_BASE = {"PA": 0, "PB": 2, "PC": 4, "PD": 6, "PT": 7}

    def banks(t, c0=0, c1=None):
        base = PSUM_BASE[t.name]
        if t.name in ("PD", "PT"):
            return (base,)
        c1 = 1024 if c1 is None else c1
        return tuple(range(base + c0 // 512, base + (c1 + 511) // 512))

    NIS = min(4, NSL)
    ISW = T // NIS
    for k4 in range(NIS):
        dstR = bass.AP(R, k4 * ISW, [[NCH * T, 128], [T, NCH], [1, ISW]])
        srcX = bass.AP(xT, k4 * ISW, [[T, 128], [128 * T, NCH], [1, ISW]])
        p.dma("sp", dstR, srcX, f"d_i{k4}")
    for dst, src in ((IDN, identb), (MSK, maskb4), (ONE, onesc_h), (BIA, bias1_h)):
        p.dma("sp", dst[:], src[:, :], "d_in")
    t_in = p.val("d_in")
    sbr.wrote(["R"], 16, f"d_i{NIS - 1}")
    sbr.wrote(["IDN", "MSK", "ONE", "BIA"], t_in, "d_in")
    first_ln = [True]

    p.mset("dve", SM[:, SM_C20:SM_C20 + 1], SOFT_SHIFT, inc="c_dve")
    sbr.wrote(["C20"], p.val("c_dve"), "c_dve")
    p.mset("dve", SM[:, SM_EPS:SM_EPS + 1], EPS_ATT, inc="c_dve")
    sbr.wrote(["EPSC"], p.val("c_dve"), "c_dve")
    p.mset("dve", SM[:, SM_EPSL:SM_EPSL + 1], EPS_LN, inc="c_dve")
    sbr.wrote(["EPSL"], p.val("c_dve"), "c_dve")
    p.mset("pool", LN1S[:], 1.0, inc="c_pool")
    sbr.wrote(["LN1S"], p.val("c_pool"), "c_pool")
    H2 = HID // 2
    cast_plan = [("gT0", None, 0, 256, "dw_g0f"), ("uT0", None, 0, 256, "dw_u0f"),
                 ("gT0", None, 256, H2 - 256, "dw_gT0"),
                 ("uT0", None, 256, H2 - 256, "dw_uT0"),
                 ("dT0", (0, HID // 2), 0, D, "dw_dT0"),
                 ("gT0", None, H2, H2, "dw_gT0b"), ("uT0", None, H2, H2, "dw_uT0b"),
                 ("dT0", (HID // 2, HID // 2), 0, D, "dw_dT0b"),
                 ("q0T", None, 0, D, "dw_q0T"), ("k0T", None, 0, D, "dw_k0T"),
                 ("v0T", None, 0, D, "dw_v0T"), ("o0T", None, 0, D, "dw_o0T"),
                 ("gT1", None, 0, H2, "dw_gT1"), ("uT1", None, 0, H2, "dw_uT1"),
                 ("dT1", (0, HID // 2), 0, D, "dw_dT1"),
                 ("gT1", None, H2, H2, "dw_gT1b"), ("uT1", None, H2, H2, "dw_uT1b"),
                 ("dT1", (HID // 2, HID // 2), 0, D, "dw_dT1b"),
                 ("q1", None, 0, D, "dw_q1"), ("k1", None, 0, D, "dw_k1"),
                 ("v1", None, 0, D, "dw_v1")]
    prev_cast = None
    for nm, rows, c0, cols, sem in cast_plan:
        if prev_cast:
            p.wait("pool", prev_cast, 16)
        if rows is None:
            p.dma("pool", w_b[nm][:, c0:c0 + cols], w_raw[nm][:, c0:c0 + cols],
                  sem, mdld=8192)
        else:
            r0, rn = rows
            p.dma("pool", w_b[nm][r0:r0 + rn, :], w_raw[nm][r0:r0 + rn, :],
                  sem, mdld=8192)
        prev_cast = sem

    ln_cast_done = {}
    ln_done = set()
    HOOKP = [False]
    APPLY_TOK = {}

    def ln_cast(k4, hooked=False):
        if k4 in ln_cast_done:
            return
        sbr.w.pop(f"XNa{k4}", None)
        sbr.r.pop(f"XNa{k4}", None)
        for c in range(NCH):
            XNA_TOK.pop((k4, c), None)
        csl = bass.AP(XN, k4 * SL, [[NCH * T, 128], [T, NCH], [1, SL]])
        rsl = bass.AP(R, k4 * SL, [[NCH * T, 128], [T, NCH], [1, SL]])
        e = {"sq": None, "ap": None, "keys": None}
        if hooked:
            p.wait("dve", "c_pe", p.val("c_pe"))
            p.wait("dve", "c_dve", p.val("c_dve"))
        else:
            sbr.reading("dve", ["R"])
            sbr.acquire("dve", ["XN"])
            if first_ln[0]:
                p.wait("dve", f"d_i{min(k4 * SL // ISW, NIS - 1)}", 16)
        e["xc"] = p.cp("dve", csl, rsl, inc="c_dve")
        sbr.wrote(["XN"], e["xc"], "c_dve")
        ln_cast_done[k4] = e
        if not hooked:
            ln_square(k4)

    def ln_square(k4):
        e = ln_cast_done[k4]
        sbr.reading("act", ["R"])
        sbr.acquire("act", ["SBA"])
        if first_ln[0]:
            p.wait("act", f"d_i{min(k4 * SL // ISW, NIS - 1)}", 16)
        rsl = bass.AP(R, k4 * SL, [[NCH * T, 128], [T, NCH], [1, SL]])
        qsl = bass.AP(SBA, k4 * SL, [[SA_COLS, 128], [T, NCH], [1, SL]])
        e["sq"] = p.actv(qsl, rsl, AF.Square, inc="c_act")
        e["ap"] = lambda c, tl: SBA[:, c * T + k4 * SL + tl * 128:
                                    c * T + k4 * SL + tl * 128 + 128]
        e["keys"] = ["SBA"]
        sbr.wrote(["SBA"], e["sq"], "c_act")

    def ln_hook_sq(k2, win, eng, r_tok=None):
        e = ln_cast_done[k2]
        if e["sq"] is not None:
            return
        buf, base, pitch, key = win
        p.wait(eng, "c_pe", p.val("c_pe"))
        p.wait(eng, "c_dve", r_tok if r_tok is not None else p.val("c_dve"))
        sbr.acquire(eng, [key])
        rsl = bass.AP(R, k2 * SL, [[NCH * T, 128], [T, NCH], [1, SL]])
        wsl = bass.AP(buf, base, [[pitch, 128], [SL, NCH], [1, SL]])
        if eng == "act":
            e["sq"] = p.actv(wsl, rsl, AF.Square, inc="c_act")
            e["sqsem"] = "c_act"
        else:
            e["sq"] = p.tt("dve", wsl, rsl, rsl, ALU.mult, inc="c_dve")
            e["sqsem"] = "c_dve"
        e["ap"] = lambda c, tl, buf=buf, base=base: \
            buf[:, base + c * SL + tl * 128: base + c * SL + tl * 128 + 128]
        e["keys"] = [key]
        sbr.wrote([key], e["sq"], e["sqsem"])

    def ln_full(k2):
        ln_stats(k2)
        apply_ln(k2)

    def ln_stats(k4):
        if k4 in ln_done or k4 not in ln_cast_done:
            return
        ln_done.add(k4)
        e = ln_cast_done[k4]
        if e["sq"] is None:
            ln_square(k4)
        ps.acquire("pe", banks(PD))
        sbr.reading("pe", ["ONE"])
        p.wait("pe", "c_dve", e["xc"])
        p.wait("pe", e.get("sqsem", "c_act"), e["sq"])
        t_stats = None
        for tl in range(TPSL):
            tt = k4 * TPSL + tl
            for c in range(NCH):
                t_stats = p.mm(PD[:, tt:tt + 1],
                               XN[:, c * T + tt * 128: c * T + tt * 128 + 128],
                               ONE[:], c == 0, c == NCH - 1,
                               inc="c_pe" if c == NCH - 1 else None) or t_stats
            for c in range(NCH):
                t_stats = p.mm(PD[:, NT + tt:NT + tt + 1], e["ap"](c, tl),
                               ONE[:], c == 0, c == NCH - 1,
                               inc="c_pe" if c == NCH - 1 else None) or t_stats
        sbr.read_done(e["keys"], "c_pe", t_stats)
        ps.wrote(banks(PD), t_stats)
        a0 = SM_MSQ + k4 * TPSL
        v0 = SM_VAR + k4 * TPSL
        r0 = SM_RS + k4 * TPSL
        b0 = SM_BS + k4 * TPSL
        d0 = k4 * TPSL
        if k4 in APPLY_TOK:
            ta_k, tb_k = APPLY_TOK.pop(k4)
            p.wait("act", "c_dve", ta_k)
            p.wait("dve", "c_act", tb_k)
        ps.reading("act", banks(PD))
        t1 = p.actv(SM[:, a0:a0 + TPSL], PD[:, d0:d0 + TPSL], AF.Square,
                    scale=1.0 / D, inc="c_act")
        ps.reading("dve", banks(PD))
        p.wait("dve", "c_act", t1)
        t2 = p.stt(SM[:, v0:v0 + TPSL], PD[:, NT + d0:NT + d0 + TPSL], 1.0 / D,
                   SM[:, a0:a0 + TPSL], ALU.mult, ALU.subtract, inc="c_dve")
        p.wait("act", "c_dve", t2)
        sbr.reading("act", ["EPSL"])
        t3 = p.actv(SM[:, a0:a0 + TPSL], SM[:, v0:v0 + TPSL], AF.Ln,
                    bias=SM[:, SM_EPSL:SM_EPSL + 1], inc="c_act")
        p.wait("act", "c_act", t3)
        t4 = p.actv(SM[:, r0:r0 + TPSL], SM[:, a0:a0 + TPSL], AF.Exp,
                    scale=-0.5, inc="c_act")
        p.wait("dve", "c_act", t4)
        t5 = p.stt(SM[:, b0:b0 + TPSL], PD[:, d0:d0 + TPSL], -1.0 / D,
                   SM[:, r0:r0 + TPSL], ALU.mult, ALU.mult, inc="c_dve")
        ps.read_done(banks(PD), "c_act", t1)
        ps.read_done(banks(PD), "c_dve", t5)
        sbr.wrote([f"LNRS{k4}"], t4, "c_act")
        sbr.wrote([f"LNBS{k4}"], t5, "c_dve")

    def phase_ln():
        for k4 in range(NSL):
            ln_cast(k4)
            if k4 > 0:
                ln_stats(k4 - 1)
                apply_ln(k4 - 1, pre=True)
        ln_stats(NSL - 1)
        apply_ln(NSL - 1, pre=True)
        first_ln[0] = False
        ln_done.clear()
        ln_cast_done.clear()

    def apply_ln(k, pre=False, pt=False):
        if f"XNa{k}" in sbr.w:
            return
        sbr.wrote([f"XNa{k}"], 0, "c_dve")
        if pt:
            ab = PT
            a_bk, a_ap = banks(PT), PT[:, 0:SL]
            b_bk, b_ap = banks(PT), PT[:, 512:512 + SL]
            b_off = 512
        elif pre:
            ab = (PA, PB)[k % 2]
            a_bk, a_ap = banks(ab, 0, 512), ab[:, 0:SL]
            b_bk, b_ap = banks(ab, 512, 1024), ab[:, 512:512 + SL]
            b_off = 512
        else:
            ab = None
            a_bk, a_ap = banks(PD), PD[:, 0:SL]
            b_bk, b_ap = banks(PC, 0, 512), PC[:, 0:SL]
            b_off = 0
        sbr.reading("dve", [f"LNRS{k}", "IDN"])
        sbr.reading("act", [f"LNBS{k}", "IDN"])
        ps.acquire("pe", a_bk + b_bk)
        sbr.acquire("dve", ["DTA"])
        sbr.acquire("act", ["DTB"])
        ta = tb = None
        for tl in range(TPSL):
            tt = k * TPSL + tl
            ta = p.ts("dve", DTILE[:, tl * 128:tl * 128 + 128], IDN[:],
                      SM[:, SM_RS + tt:SM_RS + tt + 1], None, ALU.mult, inc="c_dve")
            tb = p.actv(DTILE[:, 512 + tl * 128:512 + tl * 128 + 128], IDN[:],
                        AF.Copy, scale=SM[:, SM_BS + tt:SM_BS + tt + 1], inc="c_act")
        sbr.wrote(["DTA"], ta, "c_dve")
        sbr.wrote(["DTB"], tb, "c_act")
        APPLY_TOK[k] = (ta, tb)
        p.wait("pe", "c_dve", ta)
        p.wait("pe", "c_act", tb)
        sbr.reading("pe", ["LN1S"])
        b1 = b2 = None
        for tl in range(TPSL):
            b1 = p.mm((PD if not pre else ab)[:, tl * 128:tl * 128 + 128], LN1S[:],
                      DTILE[:, tl * 128:tl * 128 + 128], True, True, inc="c_pe")
            b2 = p.mm((PC if not pre else ab)[:, b_off + tl * 128:
                                              b_off + tl * 128 + 128], LN1S[:],
                      DTILE[:, 512 + tl * 128:512 + tl * 128 + 128], True, True,
                      inc="c_pe")
        sbr.read_done(["DTA", "DTB"], "c_pe", b2)
        ps.wrote(a_bk, b2)
        ps.wrote(b_bk, b2)
        ps.reading("act", a_bk + b_bk)
        tea = p.actv(DTILE[:, 0:SL], a_ap, AF.Copy, inc="c_act")
        teb = p.actv(DTILE[:, 512:512 + SL], b_ap, AF.Copy, inc="c_act")
        ps.read_done(a_bk + b_bk, "c_act", teb)
        sbr.wrote(["DTA"], tea, "c_act")
        sbr.wrote(["DTB"], teb, "c_act")
        sbr.reading("dve", ["DTA", "DTB", "XN"])
        last = None
        for c in range(NCH):
            sl = slice(c * T + k * SL, c * T + k * SL + SL)
            t1 = p.tt("dve", XN[:, sl], XN[:, sl], DTILE[:, 0:SL], ALU.mult,
                      inc="c_dve")
            p.wait("dve", "c_dve", t1)
            last = p.tt("dve", XN[:, sl], XN[:, sl], DTILE[:, 512:512 + SL],
                        ALU.add, inc="c_dve")
            XNA_TOK[(k, c)] = last
        sbr.read_done(["DTA", "DTB"], "c_dve", last)
        sbr.wrote([f"XNa{k}"], last, "c_dve")

    def pf_ffn(l):
        gB, uB, dB = w_b[f"gT{l}"], w_b[f"uT{l}"], w_b[f"dT{l}"]
        pairs = [(j, hf, hp) for j in range(NSF) for hf in range(2)
                 for hp in range(HHL // 2)]
        slab_tok = {}

        def issue_slab(i):
            if i >= len(pairs):
                return
            _, hf, hp = pairs[i]
            hpg = hf * (HHL // 2) + hp
            piece = "" if hpg < HHL // 2 else "b"
            if l == 0 and hpg == 0:
                p.wait("sp", "dw_g0f", 16)
                p.wait("sp", "dw_u0f", 16)
            else:
                p.wait("sp", f"dw_gT{l}{piece}", 16)
                p.wait("sp", f"dw_uT{l}{piece}", 16)
            slot = i % 2
            gcol, ucol = slot * 4096, slot * 4096 + 2048
            qf_sem = ("q_f0", "q_f1")[slot]
            sbr.acquire("sp", [f"WG{slot}", f"WU{slot}"] + ["WBUF", "WG0", "WU0", "WG1", "WU1"])
            p.dma("sp", WBUF[:, gcol:gcol + 2048].rearrange("p (c n) -> p c n", c=NCH),
                  gB[:, hpg * 256:hpg * 256 + 256]
                  .rearrange("(c p) n -> p c n", p=128), qf_sem)
            p.dma("sp", WBUF[:, ucol:ucol + 2048].rearrange("p (c n) -> p c n", c=NCH),
                  uB[:, hpg * 256:hpg * 256 + 256]
                  .rearrange("(c p) n -> p c n", p=128), qf_sem)
            slab_tok[i] = p.val(qf_sem)
            sbr.wrote([f"WG{slot}", f"WU{slot}"], slab_tok[i], qf_sem)

        wd_st = {"pend": [], "first": False, "piece": "", "tok": 0}

        def wd_queue(hf):
            wd_st["piece"] = "" if hf == 0 else "b"
            wd_st["first"] = True
            wd_st["pend"] = [0, 1, 2]

        def wd_drain(n):
            while n > 0 and wd_st["pend"]:
                dgl = wd_st["pend"].pop(0)
                if wd_st["first"]:
                    p.wait("sp", f"dw_dT{l}{wd_st['piece']}", 16)
                    sbr.acquire("sp", ["WD", "OTBUF", "AMT0", "AMT1", "KFH",
                                       "QS1", "PTR", "OTK"])
                    wd_st["first"] = False
                rows = (HHL * 128 if wd_st["piece"] else 0) + dgl * 512
                p.dma("sp",
                      SBC[:, dgl * 4096:dgl * 4096 + 4096]
                      .rearrange("p (c n) -> p c n", c=4),
                      dB[rows:rows + 512, :].rearrange("(c p) n -> p c n", p=128),
                      "q_wd")
                n -= 1
            if not wd_st["pend"]:
                wd_st["tok"] = p.val("q_wd")
                sbr.wrote(["WD"], wd_st["tok"], "q_wd")

        issue_slab(0)
        issue_slab(1)
        wd_queue(0)
        wd_drain(1)
        return {"issue_slab": issue_slab, "slab_tok": slab_tok,
                "wd_queue": wd_queue, "wd_drain": wd_drain, "wd_st": wd_st}

    def phase_ffn(l, st):
        issue_slab, slab_tok = st["issue_slab"], st["slab_tok"]
        wd_queue, wd_drain, wd_st = st["wd_queue"], st["wd_drain"], st["wd_st"]
        sbr.reading("pe", ["XN"])
        rot_state = [0]
        pi = 0
        pend_cast = []
        pend_sq = []
        pend_tr = [None]
        spj = max(1, SF // SL)

        def ffn_win(k2):
            return (SBB, 1024 + (k2 % spj) * NCH * SL, SB_COLS,
                    f"LNW{k2 % spj}")

        for j in range(NSF):
            for k in range(j * SF // SL, (j + 1) * SF // SL):
                apply_ln(k)
            for k2 in pend_cast:
                ln_cast(k2, hooked=True)
            pend_cast = []
            if pend_sq:
                ln_hook_sq(pend_sq[0], ffn_win(pend_sq[0]), "act", pend_tr[0])
            for hf in range(2):
                if not (j == 0 and hf == 0):
                    wd_queue(hf)
                for hp in range(HHL // 2):
                    slot = pi % 2
                    gcol, ucol = slot * 4096, slot * 4096 + 2048
                    qf_sem = ("q_f0", "q_f1")[slot]
                    t_slab = slab_tok[pi]
                    for hh in range(2):
                        hcl = hp * 2 + hh
                        p.wait("pe", qf_sem, t_slab)
                        for half in range(max(1, SF // 512)):
                            w = min(512, SF)
                            kk = (j * SF + half * 512) // SL
                            gb, go = ((PA, 0), (PA, 512), (PB, 0), (PB, 512))[
                                rot_state[0] % 4]
                            ub, uo = ((PC, 0), (PC, 512), (PD, 0))[rot_state[0] % 3]
                            rot_state[0] += 1
                            t_g = t_u = None
                            for col, pt, po in ((gcol, gb, go), (ucol, ub, uo)):
                                ps.acquire("pe", banks(pt, po, po + 512))
                                for dc in range(NCH):
                                    p.wait("pe", "c_dve", XNA_TOK.get((kk, dc), 0))
                                    tk = p.mm(
                                        pt[:, po:po + w],
                                        WBUF[:, col + dc * 256 + hh * 128:
                                             col + dc * 256 + hh * 128 + 128],
                                        XN[:, dc * T + j * SF + half * 512:
                                           dc * T + j * SF + half * 512 + w],
                                        dc == 0, dc == NCH - 1,
                                        inc="c_pe" if dc == NCH - 1 else None)
                                    if tk:
                                        if pt is gb and po == go:
                                            t_g = tk
                                        else:
                                            t_u = tk
                            sbr.read_done([f"WG{slot}", f"WU{slot}"], "c_pe", t_u)
                            ps.wrote(banks(gb, go, go + 512), t_g)
                            ps.wrote(banks(ub, uo, uo + 512), t_u)
                            ps.reading("act", banks(gb, go, go + 512))
                            sbr.acquire("act", [f"FTMP{half}"])
                            tsl = p.actv(SBB[:, half * 512:half * 512 + w],
                                         gb[:, go:go + w], AF.Silu, scale=INV127,
                                         inc="c_act")
                            sbr.wrote([f"FTMP{half}"], tsl, "c_act")
                            ps.read_done(banks(gb, go, go + 512), "c_act", tsl)
                            ps.reading("dve", banks(ub, uo, uo + 512))
                            p.wait("dve", "c_act", tsl)
                            sbr.acquire("dve", [f"H{hcl}_{half}"])
                            th = p.tt("dve",
                                      SBA[:, hcl * SF + half * 512:
                                          hcl * SF + half * 512 + w],
                                      SBB[:, half * 512:half * 512 + w],
                                      ub[:, uo:uo + w], ALU.mult, inc="c_dve")
                            sbr.wrote([f"H{hcl}_{half}"], th, "c_dve")
                            sbr.read_done([f"FTMP{half}"], "c_dve", th)
                            ps.read_done(banks(ub, uo, uo + 512), "c_dve", th)
                    issue_slab(pi + 2)
                    wd_drain(1)
                    if hf == 0 and pend_sq:
                        if hp == 0:
                            if len(pend_sq) > 1:
                                ln_hook_sq(pend_sq[1], ffn_win(pend_sq[1]),
                                           "act", pend_tr[0])
                            ln_full(pend_sq[0])
                        elif hp == 1:
                            for k2 in pend_sq[1:]:
                                ln_full(k2)
                            pend_sq = []
                    pi += 1
                hkeys = [f"H{hcl}_{hf2}" for hcl in range(HHL)
                         for hf2 in range(max(1, SF // 512))]
                wd_drain(3)
                sbr.reading("pe", ["WD"] + hkeys)
                p.wait("pe", "q_wd", wd_st["tok"])
                for do in range(NCH):
                    pt = (PA, PB)[do % 2]
                    ps.acquire("pe", banks(pt, 0, SF))
                    tk = None
                    for half in range(max(1, SF // 512)):
                        w = min(512, SF)
                        for hcl in range(HHL):
                            tk = p.mm(
                                pt[:, half * 512:half * 512 + w],
                                SBC[:, hcl * 1024 + do * 128: hcl * 1024 + do * 128 + 128],
                                SBA[:, hcl * SF + half * 512: hcl * SF + half * 512 + w],
                                hcl == 0, hcl == HHL - 1,
                                inc="c_pe" if hcl == HHL - 1 else None) or tk
                    ps.wrote(banks(pt, 0, SF), tk)
                    if do == NCH - 1:
                        sbr.read_done(["WD"] + hkeys, "c_pe", tk)
                    ps.reading("dve", banks(pt, 0, SF))
                    sbr.reading("dve", ["R"])
                    rsl = slice(do * T + j * SF, do * T + j * SF + SF)
                    tr = p.stt(R[:, rsl], pt[:, 0:SF], S_DOWN, R[:, rsl],
                               ALU.mult, ALU.add, inc="c_dve")
                    ps.read_done(banks(pt, 0, SF), "c_dve", tr)
                    if st.get("final") and j == NSF - 1 and hf == 1:
                        p.wait("sp", "c_dve", tr)
                        p.dma("sp", yT[do * 128:(do + 1) * 128, :],
                              R[:, do * T:(do + 1) * T], "d_out")
                        st["out_emitted"] = True
                if hf == 1 and HOOKP[0]:
                    pend_cast = list(range(j * spj, (j + 1) * spj))
                    pend_sq = list(pend_cast)
                    pend_tr[0] = tr
        for k2 in pend_cast:
            ln_cast(k2, hooked=True)
        for k2 in pend_sq:
            ln_hook_sq(k2, ffn_win(k2), "act", pend_tr[0])
        sbr.wrote(["R"], p.val("c_dve"), "c_dve")

    tmp_state = {"cur": 0}
    SPTMP = 2 * NCK * 1024

    def softplus_evac(pt, c0, w, dbuf, dcol):
        cur = tmp_state["cur"]
        tmp_state["cur"] ^= 1
        tcol = (SPTMP, SPTMP + 512)[cur]
        tkey = f"SPT{cur}"
        ps.reading("act", banks(pt, c0, c0 + w))
        sbr.acquire("act", [tkey])
        ta = p.actv(SBB[:, tcol:tcol + w], pt[:, c0:c0 + w], AF.Exp, inc="c_act")
        p.wait("act", "c_act", ta)
        tb = p.actv(SBB[:, tcol:tcol + w], SBB[:, tcol:tcol + w], AF.Ln, bias=1.0,
                    inc="c_act")
        sbr.wrote([tkey], tb, "c_act")
        ps.read_done(banks(pt, c0, c0 + w), "c_act", ta)
        p.wait("dve", "c_act", tb)
        tc = p.ts("dve", dbuf[:, dcol:dcol + w], SBB[:, tcol:tcol + w], EPS_ATT, None,
                  ALU.add, inc="c_dve")
        sbr.read_done([tkey], "c_dve", tc)
        return tc

    def load_w_striped(nm, dbuf=None, dbase=0, dstride=8192, sems="q_s",
                       keys=("WBUF", "WG0", "WU0", "WG1", "WU1"), per_stripe=False):
        dbuf = WBUF if dbuf is None else dbuf
        p.wait("sp", f"dw_{nm}", 16)
        if not per_stripe:
            sbr.acquire("sp", list(keys))
        toks = []
        for st4 in range(4):
            if per_stripe:
                sbr.acquire("sp", [f"WS{st4}"] + list(keys))
            dst = bass.AP(dbuf, dbase + st4 * 256, [[dstride, 128], [1024, NCH], [1, 256]])
            p.dma("sp", dst,
                  w_b[nm][:, st4 * 256:st4 * 256 + 256]
                  .rearrange("(c p) n -> p c n", p=128), f"{sems}{st4}")
            toks.append(p.val(f"{sems}{st4}"))
            if per_stripe:
                sbr.wrote([f"WS{st4}"], toks[-1], f"{sems}{st4}")
        sbr.wrote([keys[0]], toks[-1], f"{sems}3")
        return toks

    def pf_l0():
        return {"t_wq0": load_w_striped("q0T")}

    def phase_lattn(st):
        import os
        _cut = int(os.environ.get("MK_L0CUT", "0"))
        _scut = int(os.environ.get("MK_SCANCUT", "9"))
        QFV, KFV, KTOK, QFH_LO = 0, 8 * SAT, 16 * SAT, 16 * SAT + NCK * 1024
        VTK, OTOK = 0, NCK * 1024
        QFH_HI = SPTMP + 1024
        SPB = QFH_HI + 8 * SAT
        OT, AMT, KFH = 0, 8 * SAT, 8 * SAT + 1024

        def qfh_ap(h, c0, w):
            if h < 8:
                return SBA[0:64, QFH_LO + h * SAT + c0: QFH_LO + h * SAT + c0 + w]
            return SBB[0:64, QFH_HI + (h - 8) * SAT + c0: QFH_HI + (h - 8) * SAT + c0 + w]

        sbr.acquire("dve", ["SPB"])
        tz = p.mset("dve", SBB[0:64, SPB:SPB + 1040], 0.0, inc="c_dve")
        sbr.wrote(["SPB"], tz, "c_dve")
        sbr.reading("dve", ["MSK"])
        sbr.reading("pe", ["XN", "ONE", "IDN"])

        t_wq_next = st["t_wq0"]
        for j in range(NSA):
            apply_ln(j)
            if HOOKP[0] and j > 0:
                ln_cast(j - 1, hooked=True)
                ln_hook_sq(j - 1, (SBA, KTOK, SA_COLS, "KTB"), "act")
            t_wq = t_wq_next

            ROT6 = ((PA, 0), (PA, 512), (PB, 0), (PB, 512), (PC, 0), (PC, 512))

            def proj_feat_sp(t_w, dcol_base, wkey):
                tlast = None
                for ec in range(NCH):
                    bank, boff = ROT6[ec % 6]
                    ps.acquire("pe", banks(bank, boff, boff + 512))
                    p.wait("pe", f"q_s{ec // 2}", t_w[ec // 2])
                    tk = None
                    for dc in range(NCH):
                        p.wait("pe", "c_dve", XNA_TOK.get((j, dc), 0))
                        tk = p.mm(bank[:, boff:boff + SAT],
                                  WBUF[:, dc * 1024 + ec * 128:
                                       dc * 1024 + ec * 128 + 128],
                                  XN[:, dc * T + j * SAT: dc * T + j * SAT + SAT],
                                  dc == 0, dc == NCH - 1,
                                  inc="c_pe" if dc == NCH - 1 else None) or tk
                    sbr.read_done([wkey, f"WS{ec // 2}"], "c_pe", tk)
                    ps.wrote(banks(bank, boff, boff + 512), tk)
                    tlast = softplus_evac(bank, boff, SAT, SBA, dcol_base + ec * SAT)
                return tlast

            def headflat(srcbuf, srccol, dst_is_q, wkey):
                toks = []
                for half in range(2):
                    for par in range(2):
                        src = bass.AP(srcbuf, par * 64 * SA_COLS + srccol
                                      + half * 4 * SAT,
                                      [[SA_COLS, 64], [SAT, 4], [1, SAT]])
                        h0 = half * 8 + par
                        if dst_is_q:
                            if h0 < 8:
                                dbuf, dcol = SBA, QFH_LO + h0 * SAT
                                dstride = SA_COLS
                            else:
                                dbuf, dcol = SBB, QFH_HI + (h0 - 8) * SAT
                                dstride = SB_COLS
                        else:
                            dbuf, dcol = SBC, KFH + h0 * SAT
                            dstride = SC_COLS
                        dst = bass.AP(dbuf, dcol, [[dstride, 64], [2 * SAT, 4], [1, SAT]])
                        p.dma("act", dst, src, "q_hf")
                        toks.append(p.val("q_hf"))
                return toks[-1]

            sbr.acquire("pe", ["QFV"])
            tqf = proj_feat_sp(t_wq, QFV, "WBUF")
            sbr.wrote(["QFV"], tqf, "c_dve")
            if HOOKP[0] and j > 0:
                ln_full(j - 1)
            t_wk = load_w_striped("k0T", per_stripe=True)
            sbr.acquire("pe", ["KFV"])
            tkf = proj_feat_sp(t_wk, KFV, "WBUF")
            sbr.wrote(["KFV"], tkf, "c_dve")
            sbr.reading("act", ["QFV", "KFV"])
            sbr.acquire("act", ["QFH", "KFH"])
            headflat(SBA, QFV, True, "QFV")
            t_hf = headflat(SBA, KFV, False, "KFV")
            sbr.wrote(["QFH", "KFH"], t_hf, "q_hf")
            sbr.read_done(["QFV", "KFV"], "q_hf", t_hf)
            sbr.acquire("pe", ["KTB"])
            tkt = None
            ri = 2
            for tl in range(NCK):
                for half in range(2):
                    bank, boff = ROT6[ri % 6]
                    ri += 1
                    ps.acquire("pe", banks(bank, boff, boff + 512))
                    p.wait("pe", f"q_s{2 * half}", t_wk[2 * half])
                    p.wait("pe", f"q_s{2 * half + 1}", t_wk[2 * half + 1])
                    tk = None
                    for dc in range(NCH):
                        p.wait("pe", "c_dve", XNA_TOK.get((j, dc), 0))
                        tk = p.mm(bank[:, boff:boff + 512],
                                  XN[:, dc * T + j * SAT + tl * 128:
                                     dc * T + j * SAT + tl * 128 + 128],
                                  WBUF[:, dc * 1024 + half * 512:
                                       dc * 1024 + half * 512 + 512],
                                  dc == 0, dc == NCH - 1,
                                  inc="c_pe" if dc == NCH - 1 else None) or tk
                    sbr.read_done(["WBUF", f"WS{2 * half}", f"WS{2 * half + 1}"],
                                  "c_pe", tk)
                    ps.wrote(banks(bank, boff, boff + 512), tk)
                    tkt = softplus_evac(bank, boff, 512, SBA,
                                        KTOK + tl * 1024 + half * 512)
            sbr.wrote(["KTB"], tkt, "c_dve")
            t_wv = load_w_striped("v0T", per_stripe=True)
            sbr.acquire("pe", ["VTK"])
            tv_d = tv_a = None
            for tl in range(NCK):
                for half in range(2):
                    bank, boff = ROT6[ri % 6]
                    ri += 1
                    ps.acquire("pe", banks(bank, boff, boff + 512))
                    p.wait("pe", f"q_s{2 * half}", t_wv[2 * half])
                    p.wait("pe", f"q_s{2 * half + 1}", t_wv[2 * half + 1])
                    tk = None
                    for dc in range(NCH):
                        p.wait("pe", "c_dve", XNA_TOK.get((j, dc), 0))
                        tk = p.mm(bank[:, boff:boff + 512],
                                  XN[:, dc * T + j * SAT + tl * 128:
                                     dc * T + j * SAT + tl * 128 + 128],
                                  WBUF[:, dc * 1024 + half * 512:
                                       dc * 1024 + half * 512 + 512],
                                  dc == 0, dc == NCH - 1,
                                  inc="c_pe" if dc == NCH - 1 else None) or tk
                    sbr.read_done(["WBUF", "WS0", "WS1", "WS2", "WS3"], "c_pe", tk)
                    ps.wrote(banks(bank, boff, boff + 512), tk)
                    dst = SBB[:, VTK + tl * 1024 + half * 512:
                              VTK + tl * 1024 + half * 512 + 512]
                    if ri % 2 == 0:
                        ps.reading("dve", banks(bank, boff, boff + 512))
                        tv_d = p.cp("dve", dst, bank[:, boff:boff + 512],
                                    inc="c_dve")
                        ps.read_done(banks(bank, boff, boff + 512), "c_dve", tv_d)
                    else:
                        ps.reading("act", banks(bank, boff, boff + 512))
                        tv_a = p.actv(dst, bank[:, boff:boff + 512], AF.Copy,
                                      inc="c_act")
                        ps.read_done(banks(bank, boff, boff + 512), "c_act", tv_a)
            sbr.wrote(["VTK"], tv_d, "c_dve")
            if tv_a is not None:
                p.wait("pe", "c_act", tv_a)
            big = (16 * SAT >= 8192)
            if big:
                t_wo = load_w_striped("o0T", dbuf=SBA, dbase=QFV, dstride=SA_COLS,
                                      sems="q_o", keys=("QFV", "KFV"))
            else:
                t_wo = load_w_striped("o0T", sems="q_o")
            if j + 1 < NSA:
                t_wq_next = load_w_striped("q0T", per_stripe=True)

            if _cut == 1:
                continue
            sbr.reading("pe", ["QFH", "KFH", "KTB", "VTK"])
            amt_tok = {}

            def emit_half(cl, half):
                for g2 in range(2):
                    acol = g2 * 512
                    ps.acquire("pe", banks(PA, acol, acol + 512))
                    ta = None
                    for hh in range(4):
                        h = half * 8 + g2 * 4 + hh
                        ta = p.mm(PA[:, acol + hh * 128:acol + hh * 128 + 128],
                                  SBC[0:64, KFH + h * SAT + cl * 128:
                                      KFH + h * SAT + cl * 128 + 128],
                                  qfh_ap(h, cl * 128, 128),
                                  True, True,
                                  inc="c_pe" if hh == 3 else None) or ta
                    ps.wrote(banks(PA, acol, acol + 512), ta)
                ps.reading("dve", banks(PA))
                sbr.acquire("dve", ["AMT"])
                tm = p.tt("dve", SBC[:, AMT:AMT + 1024], PA[:, 0:1024], MSK[:],
                          ALU.mult, inc="c_dve")
                sbr.wrote(["AMT"], tm, "c_dve")
                ps.read_done(banks(PA), "c_dve", tm)
                amt_tok[(cl, half)] = tm

            def emit_heads_half(cl, half):
                p.wait("pe", "c_dve", amt_tok[(cl, half)])
                sbr.reading("pe", ["SPB"])
                tl_ = None
                for hh in range(8):
                    h = half * 8 + hh
                    amc = AMT + hh * 128
                    ocol = 64 * h
                    vcol = VTK + cl * 1024 + h * 64
                    scol = SPB + h * 65
                    kct = KTOK + cl * 1024 + h * 64
                    st2 = _scut < 3
                    tl_ = p.mm(PB[:, ocol:ocol + 64], SBC[:, amc:amc + 128],
                               SBB[:, vcol:vcol + 64], True, st2,
                               inc="c_pe" if st2 else None) or tl_
                    tl_ = p.mm(PC[:, h:h + 1], SBC[:, amc:amc + 128],
                               ONE[:], True, st2,
                               inc="c_pe" if st2 else None) or tl_
                    if _scut >= 3:
                        tl_ = p.mm(PB[:, ocol:ocol + 64],
                                   qfh_ap(h, cl * 128, 128),
                                   SBB[0:64, scol:scol + 64], False, True,
                                   inc=None) or tl_
                        tl_ = p.mm(PC[:, h:h + 1],
                                   qfh_ap(h, cl * 128, 128),
                                   SBB[0:64, scol + 64:scol + 65], False, True,
                                   inc="c_pe" if hh == 7 and _scut < 4
                                   else None) or tl_
                    if _scut >= 4:
                        if h < 8:
                            dsb, dsc = PC, 512 + 64 * h
                        else:
                            dsb, dsc = PD, 64 * (h - 8)
                        tl_ = p.mm(dsb[0:64, dsc:dsc + 64],
                                   SBA[:, kct:kct + 64], SBB[:, vcol:vcol + 64],
                                   True, True, inc=None) or tl_
                        tl_ = p.mm(PC[0:64, 16 + h:16 + h + 1],
                                   SBA[:, kct:kct + 64], ONE[:],
                                   True, True,
                                   inc="c_pe" if hh == 7 else None) or tl_
                sbr.read_done(["AMT"], "c_pe", tl_)
                return tl_

            def emit_back(cl):
                ps.reading("act", banks(PC, 0, 512))
                sbr.acquire("act", ["DT", "RQ"])
                sbr.reading("act", ["EPSC"])
                td1 = p.actv(SM[:, SM_DT:SM_DT + 16], PC[:, 0:16], AF.Ln,
                             bias=SM[:, SM_EPS:SM_EPS + 1], inc="c_act")
                p.wait("act", "c_act", td1)
                td2 = p.actv(SM[:, SM_RQ:SM_RQ + 16], SM[:, SM_DT:SM_DT + 16], AF.Exp,
                             scale=-1.0, inc="c_act")
                sbr.wrote(["DT"], td1, "c_act")
                sbr.wrote(["RQ"], td2, "c_act")
                ps.reading("act", banks(PB, 0, 512))
                ps.reading("dve", banks(PB, 512, 1024))
                sbr.acquire("act", ["OTOK"])
                sbr.acquire("dve", ["OTOK2"])
                sbr.reading("act", ["RQ"])
                sbr.reading("dve", ["RQ"])
                toa = tod = None
                for h in range(NHEAD):
                    ocol = 64 * h
                    dcol = OTOK + cl * 1024 + h * 64
                    if h < 8:
                        toa = p.actv(SBB[:, dcol:dcol + 64], PB[:, ocol:ocol + 64],
                                     AF.Copy, scale=SM[:, SM_RQ + h:SM_RQ + h + 1],
                                     inc="c_act")
                    else:
                        tod = p.ts("dve", SBB[:, dcol:dcol + 64], PB[:, ocol:ocol + 64],
                                   SM[:, SM_RQ + h:SM_RQ + h + 1], None, ALU.mult,
                                   inc="c_dve")
                sbr.read_done(["RQ"], "c_act", toa)
                sbr.read_done(["RQ"], "c_dve", tod)
                ps.read_done(banks(PB, 0, 512), "c_act", toa)
                ps.read_done(banks(PB, 512, 1024), "c_dve", tod)
                sbr.wrote(["OTOK"], toa, "c_act")
                sbr.wrote(["OTOK2"], tod, "c_dve")
                ps.reading("dve", banks(PC, 512, 1024) + banks(PD) + banks(PC, 0, 512))
                sbr.acquire("dve", ["SPB"])
                ts1 = p.tt("dve",
                           bass.AP(SBB, SPB, [[SB_COLS, 64], [65, 8], [1, 64]]),
                           bass.AP(SBB, SPB, [[SB_COLS, 64], [65, 8], [1, 64]]),
                           bass.AP(PC, 512, [[1024, 64], [64, 8], [1, 64]]),
                           ALU.add, inc="c_dve")
                p.wait("dve", "c_dve", ts1)
                ts2 = p.tt("dve",
                           bass.AP(SBB, SPB + 8 * 65, [[SB_COLS, 64], [65, 8], [1, 64]]),
                           bass.AP(SBB, SPB + 8 * 65, [[SB_COLS, 64], [65, 8], [1, 64]]),
                           bass.AP(PD, 0, [[512, 64], [64, 8], [1, 64]]),
                           ALU.add, inc="c_dve")
                p.wait("dve", "c_dve", ts2)
                ts3 = p.tt("dve",
                           bass.AP(SBB, SPB + 64, [[SB_COLS, 64], [65, 16]]),
                           bass.AP(SBB, SPB + 64, [[SB_COLS, 64], [65, 16]]),
                           bass.AP(PC, 16, [[1024, 64], [1, 16]]),
                           ALU.add, inc="c_dve")
                ps.read_done(banks(PC) + banks(PD), "c_dve", ts3)
                sbr.wrote(["SPB"], ts3, "c_dve")

            def emit_tp(cl):
                ps.acquire("pe", banks(PT))
                sbr.reading("pe", ["OTOK", "OTOK2", "IDN"])
                tt6 = None
                for ec in range(NCH):
                    tt6 = p.tp_(PT[:, ec * 128:ec * 128 + 128],
                                SBB[:, OTOK + cl * 1024 + ec * 128:
                                    OTOK + cl * 1024 + ec * 128 + 128],
                                IDN[:],
                                inc="c_pe" if ec == NCH - 1 else None) or tt6
                sbr.read_done(["OTOK", "OTOK2"], "c_pe", tt6)
                ps.wrote(banks(PT), tt6)
                ps.reading("act", banks(PT))
                sbr.acquire("act", ["OTBUF"])
                dst = bass.AP(SBC, OT + cl * 128, [[SC_COLS, 128], [SAT, NCH], [1, 128]])
                src = bass.AP(PT, 0, [[1024, 128], [128, NCH], [1, 128]])
                tt7 = p.actv(dst, src, AF.Copy, inc="c_act")
                sbr.wrote(["OTBUF"], tt7, "c_act")
                ps.read_done(banks(PT), "c_act", tt7)

            if _scut >= 2:
                emit_half(0, 0)
                for cl in range(NCK):
                    ps.acquire("pe", banks(PB) + banks(PC) + banks(PD))
                    t1_ = emit_heads_half(cl, 0)
                    emit_half(cl, 1)
                    t2_ = emit_heads_half(cl, 1)
                    t_all = p.val("c_pe")
                    ps.wrote(banks(PB) + banks(PC) + banks(PD), t_all)
                    if cl + 1 < NCK:
                        emit_half(cl + 1, 0)
                    if _scut >= 5:
                        emit_back(cl)
                        emit_tp(cl)
            if _cut == 2 or _scut < 5:
                continue
            sbr.read_done(["QFH", "KFH", "KTB", "VTK", "SPB"], "c_pe", p.val("c_pe"))
            if _cut == 3:
                continue
            sbr.reading("pe", ["OTBUF"])
            for fc in range(NCH):
                bank, boff = ((PA, 0), (PA, 512), (PB, 0), (PB, 512))[fc % 4]
                p.wait("pe", f"q_o{fc // 2}", t_wo[fc // 2])
                ps.acquire("pe", banks(bank, boff, boff + 512))
                tk = None
                wot = SBA if big else WBUF
                wob = QFV if big else 0
                for ec in range(NCH):
                    tk = p.mm(bank[:, boff:boff + SAT],
                              wot[:, wob + ec * 1024 + fc * 128:
                                  wob + ec * 1024 + fc * 128 + 128],
                              SBC[:, OT + ec * SAT: OT + ec * SAT + SAT],
                              ec == 0, ec == NCH - 1,
                              inc="c_pe" if ec == NCH - 1 else None) or tk
                sbr.read_done((["QFV", "KFV"] if big else ["WBUF"]) + ["OTBUF"],
                              "c_pe", tk)
                ps.wrote(banks(bank, boff, boff + 512), tk)
                ps.reading("dve", banks(bank, boff, boff + 512))
                sbr.reading("dve", ["R"])
                rsl = slice(fc * T + j * SAT, fc * T + j * SAT + SAT)
                tr = p.tt("dve", R[:, rsl], bank[:, boff:boff + SAT], R[:, rsl],
                          ALU.add, inc="c_dve")
                ps.read_done(banks(bank, boff, boff + 512), "c_dve", tr)
        if HOOKP[0]:
            ln_cast(NSA - 1, hooked=True)
            ln_hook_sq(NSA - 1, (SBA, KTOK, SA_COLS, "KTB"), "act")
        sbr.wrote(["R"], p.val("c_dve"), "c_dve")

    def pf_l1():
        p.wait("sp", "dw_k1", 16)
        sbr.acquire("sp", ["WBUF", "WG0", "WU0", "WG1", "WU1"])
        p.dma("sp", WBUF[:, 0:8192].rearrange("p (c n) -> p c n", c=NCH),
              w_b["k1"][:, :].rearrange("(c p) n -> p c n", p=128), "q_wa")
        t = p.val("q_wa")
        sbr.wrote(["WBUF"], t, "q_wa")
        return {"t_wk": t}

    def phase_sattn(st):
        KT = 0
        VT = 0
        QS, PTR, OTK = 0, 8 * SQ, 8 * SQ + NSC * SQ
        sbr.reading("pe", ["XN", "ONE", "IDN"])
        sbr.reading("dve", ["BIA"])
        for k in range(NSL):
            apply_ln(k)
        t_wk = st["t_wk"]
        sbr.acquire("pe", ["KT1"])
        p.wait("pe", "q_wa", t_wk)
        KW = min(512, T)
        tkf = None
        kidx = 0
        for ec in range(NCH):
            for st in range(T // KW):
                bank, boff = ((PA, 0), (PA, 512), (PB, 0), (PB, 512))[kidx % 4]
                kidx += 1
                ps.acquire("pe", banks(bank, boff, boff + 512))
                tk = None
                for dc in range(NCH):
                    p.wait("pe", "c_dve", XNA_TOK.get((st, dc), 0))
                    tk = p.mm(bank[:, boff:boff + KW],
                              WBUF[:, dc * 1024 + ec * 128: dc * 1024 + ec * 128 + 128],
                              XN[:, dc * T + st * KW: dc * T + st * KW + KW],
                              dc == 0, dc == NCH - 1,
                              inc="c_pe" if dc == NCH - 1 else None) or tk
                sbr.read_done(["WBUF"], "c_pe", tk)
                ps.wrote(banks(bank, boff, boff + 512), tk)
                ps.reading("dve", banks(bank, boff, boff + 512))
                tkf = p.ts("dve",
                           SBA[:, KT + ec * T + st * KW: KT + ec * T + st * KW + KW],
                           bank[:, boff:boff + KW], BIA[:, 8 + ec:8 + ec + 1], None,
                           ALU.add, inc="c_dve")
                ps.read_done(banks(bank, boff, boff + 512), "c_dve", tkf)
        sbr.wrote(["KT1"], tkf, "c_dve")
        p.wait("sp", "dw_v1", 16)
        sbr.acquire("sp", ["WBUF"])
        p.dma("sp", WBUF[:, 0:8192].rearrange("p (c n) -> p c n", c=NCH),
              w_b["v1"][:, :].rearrange("(c p) n -> p c n", p=128), "q_wa")
        t_wv = p.val("q_wa")
        sbr.wrote(["WBUF"], t_wv, "q_wa")
        sbr.acquire("pe", ["VT1"])
        p.wait("pe", "q_wa", t_wv)
        tvt = None
        for tl in range(NSC):
            bank = (PB, PC)[tl % 2]
            ps.acquire("pe", banks(bank))
            tk = None
            for half in range(2):
                for dc in range(NCH):
                    p.wait("pe", "c_dve", XNA_TOK.get((tl * 128 // SL, dc), 0))
                    tk = p.mm(bank[:, half * 512:half * 512 + 512],
                              XN[:, dc * T + tl * 128: dc * T + tl * 128 + 128],
                              WBUF[:, dc * 1024 + half * 512:
                                   dc * 1024 + half * 512 + 512],
                              dc == 0, dc == NCH - 1,
                              inc="c_pe" if dc == NCH - 1 else None) or tk
            sbr.read_done(["WBUF"], "c_pe", tk)
            ps.wrote(banks(bank), tk)
            ps.reading("dve", banks(bank))
            tvt = p.cp("dve", SBB[:, VT + tl * 1024: VT + tl * 1024 + 1024],
                       bank[:, 0:1024], inc="c_dve")
            ps.read_done(banks(bank), "c_dve", tvt)
        sbr.wrote(["VT1"], tvt, "c_dve")
        p.wait("sp", "dw_q1", 16)
        sbr.acquire("sp", ["WBUF"])
        p.dma("sp", WBUF[:, 0:8192].rearrange("p (c n) -> p c n", c=NCH),
              w_b["q1"][:, :].rearrange("(c p) n -> p c n", p=128), "q_wa")
        t_wq = p.val("q_wa")
        sbr.wrote(["WBUF"], t_wq, "q_wa")

        for st in range(T // SQ):
            if HOOKP[0] and st > 0:
                ln_cast(st - 1, hooked=True)
                ln_hook_sq(st - 1, (SBC, PTR, SC_COLS, "PTR"), "act")
            sbr.acquire("pe", ["QS1"])
            p.wait("pe", "q_wa", t_wq)
            tqs = None
            for ec in range(NCH):
                bank, boff = ((PA, 0), (PA, 512), (PB, 0), (PB, 512))[ec % 4]
                ps.acquire("pe", banks(bank, boff, boff + 512))
                tk = None
                for dc in range(NCH):
                    p.wait("pe", "c_dve", XNA_TOK.get((st, dc), 0))
                    tk = p.mm(bank[:, boff:boff + SQ],
                              WBUF[:, dc * 1024 + ec * 128: dc * 1024 + ec * 128 + 128],
                              XN[:, dc * T + st * SQ: dc * T + st * SQ + SQ],
                              dc == 0, dc == NCH - 1,
                              inc="c_pe" if dc == NCH - 1 else None) or tk
                ps.wrote(banks(bank, boff, boff + 512), tk)
                ps.reading("dve", banks(bank, boff, boff + 512))
                tqs = p.ts("dve", SBC[:, QS + ec * SQ: QS + ec * SQ + SQ],
                           bank[:, boff:boff + SQ],
                           BIA[:, ec:ec + 1], None, ALU.add, inc="c_dve")
                ps.read_done(banks(bank, boff, boff + 512), "c_dve", tqs)
            sbr.wrote(["QS1"], tqs, "c_dve")
            if HOOKP[0] and st > 0:
                ln_full(st - 1)
            sbr.reading("pe", ["KT1", "QS1"])
            sbr.acquire("act", ["PTR"])
            tpt = None
            for sc in range(NSC):
                bank, boff = ((PA, 0), (PA, 512), (PB, 0), (PB, 512))[sc % 4]
                ps.acquire("pe", banks(bank, boff, boff + SQ))
                tk = None
                for ec in range(NCH):
                    tk = p.mm(bank[:, boff:boff + SQ],
                              SBA[:, KT + ec * T + sc * 128: KT + ec * T + sc * 128 + 128],
                              SBC[:, QS + ec * SQ: QS + ec * SQ + SQ],
                              ec == 0, ec == NCH - 1,
                              inc="c_pe" if ec == NCH - 1 else None) or tk
                ps.wrote(banks(bank, boff, boff + SQ), tk)
                ps.reading("act", banks(bank, boff, boff + SQ))
                sbr.reading("act", ["C20"])
                tpt = p.actv(SBC[:, PTR + sc * SQ: PTR + sc * SQ + SQ],
                             bank[:, boff:boff + SQ], AF.Exp,
                             bias=SM[:, SM_C20:SM_C20 + 1],
                             scale=INV_SQRT_D, inc="c_act")
                ps.read_done(banks(bank, boff, boff + SQ), "c_act", tpt)
            sbr.wrote(["PTR"], tpt, "c_act")
            sbr.reading("pe", ["PTR", "VT1"])
            NTL = SQ // 128
            pv_tok = [None] * NTL

            def emit_pv(tl):
                obank = (PC, PB)[tl % 2]
                lbank = (PD, PA)[tl % 2]
                ps.acquire("pe", banks(obank) + banks(lbank, 0, 512))
                tko = tkl = None
                for sc in range(NSC):
                    for half in range(2):
                        tko = p.mm(obank[:, half * 512:half * 512 + 512],
                                   SBC[:, PTR + sc * SQ + tl * 128:
                                       PTR + sc * SQ + tl * 128 + 128],
                                   SBB[:, VT + sc * 1024 + half * 512:
                                       VT + sc * 1024 + half * 512 + 512],
                                   sc == 0, sc == NSC - 1,
                                   inc="c_pe" if sc == NSC - 1 else None) or tko
                    tkl = p.mm(lbank[:, tl:tl + 1],
                               SBC[:, PTR + sc * SQ + tl * 128:
                                   PTR + sc * SQ + tl * 128 + 128],
                               ONE[:], sc == 0, sc == NSC - 1,
                               inc="c_pe" if sc == NSC - 1 else None) or tkl
                ps.wrote(banks(obank), tko)
                ps.wrote(banks(lbank, 0, 512), tkl)
                pv_tok[tl] = (obank, lbank, tko, tkl)

            def emit_tail(tl):
                obank, lbank, tko, tkl = pv_tok[tl]
                ps.reading("dve", banks(lbank, 0, 512))
                sbr.acquire("dve", ["RL"])
                tl1 = p.recip(SM[:, SM_RL:SM_RL + 1], lbank[:, tl:tl + 1], inc="c_dve")
                sbr.wrote(["RL"], tl1, "c_dve")
                ps.read_done(banks(lbank, 0, 512), "c_dve", tl1)
                ps.reading("dve", banks(obank))
                sbr.acquire("dve", ["OTK"])
                p.wait("dve", "c_dve", tl1)
                tdv = p.ts("dve", SBC[:, OTK:OTK + 1024], obank[:, 0:1024],
                           SM[:, SM_RL:SM_RL + 1], None, ALU.mult, inc="c_dve")
                sbr.wrote(["OTK"], tdv, "c_dve")
                sbr.read_done(["RL"], "c_dve", tdv)
                ps.read_done(banks(obank), "c_dve", tdv)
                ps.acquire("pe", banks(PT))
                p.wait("pe", "c_dve", tdv)
                tt6 = None
                for ec in range(NCH):
                    tt6 = p.tp_(PT[:, ec * 128:ec * 128 + 128],
                                SBC[:, OTK + ec * 128: OTK + ec * 128 + 128], IDN[:],
                                inc="c_pe" if ec == NCH - 1 else None) or tt6
                ps.wrote(banks(PT), tt6)
                sbr.read_done(["OTK"], "c_pe", tt6)
                ps.reading("dve", banks(PT))
                sbr.reading("dve", ["R", "BIA"])
                tra = None
                for ec in range(NCH):
                    rsl = slice(ec * T + st * SQ + tl * 128,
                                ec * T + st * SQ + tl * 128 + 128)
                    tra = p.stt(R[:, rsl], PT[:, ec * 128:ec * 128 + 128],
                                BIA[:, 16 + ec:16 + ec + 1], R[:, rsl],
                                ALU.add, ALU.add, inc="c_dve")
                ps.read_done(banks(PT), "c_dve", tra)

            emit_pv(0)
            for tl in range(NTL):
                if tl + 1 < NTL:
                    emit_pv(tl + 1)
                emit_tail(tl)
            sbr.read_done(["PTR", "VT1", "KT1", "QS1"], "c_pe", p.val("c_pe"))
        if HOOKP[0]:
            ln_cast(T // SQ - 1, hooked=True)
            ln_hook_sq(T // SQ - 1, (SBC, PTR, SC_COLS, "PTR"), "act")
        sbr.wrote(["R"], p.val("c_dve"), "c_dve")

    import os
    _ph = os.environ.get("MK_PHASES", "all")
    _plan = [("ln",), ("ffn", 0), ("ln",), ("l0",), ("ln",),
             ("ffn", 0), ("ln",), ("ffn", 1), ("ln",), ("l1",),
             ("ln",), ("ffn", 1)]
    if _ph != "all":
        _plan = _plan[:int(_ph)]
    _last_st = {}
    for _i, _step in enumerate(_plan):
        if _step[0] == "ln":
            phase_ln()
            _nxt = _plan[_i + 1] if _i + 1 < len(_plan) else None
            _st = None
            if _nxt and _nxt[0] == "ffn":
                _st = pf_ffn(_nxt[1])
                if _i + 2 == len(_plan):
                    _st["final"] = True
            elif _nxt and _nxt[0] == "l0":
                _st = pf_l0()
            elif _nxt and _nxt[0] == "l1":
                _st = pf_l1()
            barrier()
        elif _step[0] == "ffn":
            HOOKP[0] = _i + 1 < len(_plan) and _plan[_i + 1][0] == "ln"
            phase_ffn(_step[1], _st)
            _last_st = _st or {}
            barrier()
        elif _step[0] == "l0":
            HOOKP[0] = _i + 1 < len(_plan) and _plan[_i + 1][0] == "ln"
            phase_lattn(_st)
            barrier()
        else:
            HOOKP[0] = _i + 1 < len(_plan) and _plan[_i + 1][0] == "ln"
            phase_sattn(_st)
            barrier()

    if _last_st.get("out_emitted"):
        p.wait("sp", "d_out", 16 * NCH)
    else:
        sbr.reading("sp", ["R"])
        p.dma("sp", yT[:, :].rearrange("(c p) t -> p c t", p=128),
              R[:].rearrange("p (c t) -> p c t", c=NCH), "d_out")
        p.wait("sp", "d_out", 16)

    with nc.Block() as blk:
        @blk.tensor
        def _(t):
            for f in p.q["pe"]:
                f(t)

        @blk.vector
        def _(v):
            for f in p.q["dve"]:
                f(v)

        @blk.scalar
        def _(s):
            for f in p.q["act"]:
                f(s)

        @blk.gpsimd
        def _(g):
            for f in p.q["pool"]:
                f(g)

        @blk.sync
        def _(sy):
            for f in p.q["sp"]:
                f(sy)

    ctx.close()
    return nc


def _consts():
    bf = ml_dtypes.bfloat16
    identb = np.eye(128, dtype=np.float32).astype(bf)
    tri = np.triu(np.ones((128, 128), dtype=np.float32)).astype(bf)
    maskb4 = np.tile(tri, (1, 8))
    onesc = np.ones((128, 1), dtype=np.float32).astype(bf)
    return identb, maskb4, onesc


def kernel(L0_attn_x, L0_attn_Wq, L0_attn_Wk, L0_attn_Wv, L0_attn_Wo,
           L0_ffn_w_gate, L0_ffn_w_up, L0_ffn_w_down,
           L1_attn_Wq, L1_attn_bq, L1_attn_Wk, L1_attn_bk, L1_attn_Wv, L1_attn_bv,
           L1_ffn_w_gate, L1_ffn_w_up, L1_ffn_w_down):
    B, T, _ = L0_attn_x.shape
    nc = build(T)
    identb, maskb4, onesc = _consts()
    ct = np.ascontiguousarray
    bias1 = np.empty((128, 24), np.float32)
    bias1[:, 0:8] = np.asarray(L1_attn_bq, np.float32).reshape(8, 128).T
    bias1[:, 8:16] = np.asarray(L1_attn_bk, np.float32).reshape(8, 128).T
    bias1[:, 16:24] = np.asarray(L1_attn_bv, np.float32).reshape(8, 128).T
    common = {
        "identb": identb, "maskb4": maskb4, "onesc_h": onesc, "bias1_h": bias1,
        "b1v_row_h": np.asarray(L1_attn_bv, np.float32).reshape(1, D),
        "gT0": ct(np.asarray(L0_ffn_w_gate).T), "uT0": ct(np.asarray(L0_ffn_w_up).T),
        "dT0": ct(np.asarray(L0_ffn_w_down).T),
        "gT1": ct(np.asarray(L1_ffn_w_gate).T), "uT1": ct(np.asarray(L1_ffn_w_up).T),
        "dT1": ct(np.asarray(L1_ffn_w_down).T),
        "q0T": ct(np.asarray(L0_attn_Wq).T), "k0T": ct(np.asarray(L0_attn_Wk).T),
        "v0T": ct(np.asarray(L0_attn_Wv).T), "o0T": ct(np.asarray(L0_attn_Wo).T),
        "q1": ct(np.asarray(L1_attn_Wq)), "k1": ct(np.asarray(L1_attn_Wk)),
        "v1": ct(np.asarray(L1_attn_Wv)),
    }
    x = np.asarray(L0_attn_x, np.float32)
    in_maps = []
    for b in range(B):
        m = dict(common)
        m["xT"] = ct(x[b].T)
        in_maps.append(m)
    res = run_bass_kernel_spmd(nc, in_maps, core_ids=list(range(B)))
    out = np.empty((B, T, D), np.float32)
    for b in range(B):
        out[b] = np.asarray(res.results[b]["yT"]).T
    return out
